# version 07 (final)
# baseline (speedup 1.0000x reference)
import numpy as np
import ml_dtypes
import concourse.bass as bass
import concourse.mybir as mybir
from concourse.bass_utils import run_bass_kernel_spmd
from concourse.bass import IndirectOffsetOnAxis

F32 = mybir.dt.float32
BF16 = mybir.dt.bfloat16
U32 = mybir.dt.uint32
AF = mybir.ActivationFunctionType
ALU = mybir.AluOpType
AXX = mybir.AxisListType.X

D = 4096
NT = 8
TOK = NT * 128
NCHIP = 8
EPS = 1e-5
KF = 2176
NKC = KF // 128
MEAN_CNT = 3484.0


def ap2(base_ap, offset_elems, dims):
    return bass.AP(base_ap.tensor, base_ap.offset + offset_elems,
                   [[base_ap.ap[0][0], base_ap.ap[0][1]]] + dims)



def pack_panels(wt, n_o, o_width, n_q=4):
    kc_per_q = 32 // n_q
    w4 = wt.reshape(n_q, kc_per_q, 128, n_o, o_width)
    return np.ascontiguousarray(
        w4.transpose(3, 0, 2, 1, 4).reshape(n_o, n_q, 128, kc_per_q * o_width))


def pack_chunkrows(wt, width):
    return np.ascontiguousarray(
        wt.reshape(32, 128, width).transpose(1, 0, 2).reshape(128, 32 * width))


def core_rows(c):
    return np.arange(c * TOK, (c + 1) * TOK)


def make_consts():
    o = {}
    o["ident"] = np.eye(128, dtype=ml_dtypes.bfloat16)
    o["onescol"] = np.ones((1, 128), dtype=ml_dtypes.bfloat16)
    e8 = np.zeros((8, 128), np.float32)
    for e in range(8):
        e8[e, e * 16:(e + 1) * 16] = 2.0
    o["E8"] = e8.astype(ml_dtypes.bfloat16)
    e64 = np.zeros((64, 512), np.float32)
    for e in range(64):
        e64[e, e * 8:(e + 1) * 8] = 1.0
    o["E64"] = e64.astype(ml_dtypes.bfloat16)
    masks = np.zeros((4, 128, 512), np.float32)
    jj = np.arange(512)[None, :]
    pp = np.arange(128)[:, None]
    for di in range(4):
        masks[di] = np.where(jj <= di * 128 + pp, 0.0, -1e30)
    o["masks"] = masks.astype(ml_dtypes.bfloat16)
    o["iotavv"] = (np.arange(128, dtype=np.float32) * 8192.0).reshape(128, 1)
    k = np.arange(KF)
    f = np.arange(D)
    ang = 2.0 * np.pi * np.outer(f, k % D) / D
    cosm = np.cos(ang)
    sinm = np.sin(ang)
    cosm[:, 2049:] = 0.0
    sinm[:, 2049:] = 0.0
    o["COSp"] = np.ascontiguousarray(
        cosm.reshape(32, 128, NKC, 128).transpose(2, 1, 0, 3).reshape(NKC, 128, D)
    ).astype(ml_dtypes.bfloat16)
    o["SINp"] = np.ascontiguousarray(
        sinm.reshape(32, 128, NKC, 128).transpose(2, 1, 0, 3).reshape(NKC, 128, D)
    ).astype(ml_dtypes.bfloat16)
    wk = np.where((k == 0) | (k == 2048), 1.0, 2.0)
    wk[k > 2048] = 0.0
    tau = np.arange(D)
    angi = 2.0 * np.pi * np.outer(k % D, tau) / D
    icos = (wk[:, None] / D) * np.cos(angi)
    o["ICOSTp"] = np.ascontiguousarray(
        icos.reshape(NKC, 128, 32, 128).transpose(2, 1, 0, 3).reshape(32, 128, KF)
    ).astype(ml_dtypes.bfloat16)
    return o


def pack_weights(ins, c):
    w = {}
    WT = np.ascontiguousarray(ins["L0_ffn_W"].T)
    w["Wp"] = pack_panels(WT, 8, 512)
    w["Wqp"] = pack_panels(np.ascontiguousarray(ins["L0_attn_wq"]), 8, 512)
    w["Wop"] = pack_panels(np.ascontiguousarray(ins["L0_attn_wo"]), 8, 512)
    w["wkvp"] = pack_chunkrows(
        np.concatenate([ins["L0_attn_wk"], ins["L0_attn_wv"]], axis=1), 256)
    w["AallTp"] = pack_chunkrows(
        np.ascontiguousarray(ins["L0_ffn_A"].reshape(128, D).T), 128)
    w["BallT"] = np.ascontiguousarray(
        ins["L0_ffn_B"].transpose(0, 2, 1).reshape(128, D))
    w["brow"] = np.ascontiguousarray(ins["L0_ffn_b"].reshape(1, D))
    w["rWTp"] = pack_chunkrows(np.ascontiguousarray(ins["L0_ffn_router_W"].T), 8)
    w["rb2"] = np.ascontiguousarray(
        np.broadcast_to(ins["L0_ffn_router_b"].reshape(1, 8), (2, 8)))
    w["AwTp"] = pack_chunkrows(np.ascontiguousarray(ins["L1_ffn_A_w"].T), 512)
    w["BwT"] = np.ascontiguousarray(ins["L1_ffn_B_w"].T)
    w["WqTp"] = pack_chunkrows(np.ascontiguousarray(ins["L1_ffn_Wq_w"].T), 16)
    w["keysT"] = np.ascontiguousarray(ins["L1_ffn_keys"].T)
    rows = core_rows(c)
    tok = rows % 4096
    beta = c // 4
    m = ((tok >= 611) & (tok <= 4094)).astype(np.float32)
    rm = np.zeros((NT, 128, 2), np.float32)
    rm[:, :, beta] = m.reshape(NT, 128)
    w["rmask"] = np.ascontiguousarray(rm.transpose(1, 0, 2).reshape(128, NT * 2))
    bs = np.zeros((2, 1), np.float32)
    bs[beta, 0] = 1.0
    w["bselc"] = bs
    w["wqh"] = pack_chunkrows(
        np.ascontiguousarray(ins["L0_attn_wq"][:, c * 512:(c + 1) * 512]), 512)
    aoff = np.zeros((128, 64), np.uint32)
    for t in range(NT):
        for e in range(8):
            aoff[:, t * 8 + e] = (np.arange(128) + (c // 4) * 32768 + e * 4096
                                  + (c % 4) * 1024 + t * 128) * 512
    w["aoff"] = aoff
    return w



SEMS = (["qx0", "qx1", "hr", "hs", "qxr", "cq", "au", "qt", "g0", "g1", "cc",
         "wlb", "kT0", "kT1", "ct", "cs", "cv", "cg", "ga0"]
        + [f"wl{i}" for i in range(8)] + [f"ac{i}" for i in range(4)])


class Rec:
    def __init__(self, nc):
        self.nc = nc
        self.ops = []
        self.cnt = {s: 0 for s in SEMS}
        self.semh = {}

    AUTO = {"vector": "cv", "scalar": "cs", "gpsimd": "cg"}

    def do(self, eng, fn):
        sem = self.AUTO.get(eng)
        if sem is None:
            self.ops.append((eng, fn))
            return None
        return self.inc(eng, fn, sem)

    def selfwait(self, eng):
        sem = self.AUTO[eng]
        self.wait(eng, sem, self.cnt[sem])

    def inc(self, eng, fn, sem, k=1):
        h = self.semh[sem]

        def f(e, fn=fn, h=h, k=k):
            fn(e).then_inc(h, k)
        self.ops.append((eng, f))
        self.cnt[sem] += k
        return self.cnt[sem]

    def wait(self, eng, sem, val):
        if val is None or val <= 0:
            return
        h = self.semh[sem]
        self.ops.append((eng, lambda e, h=h, v=val: e.wait_ge(h, v)))

    def replay(self):
        with self.nc.Block() as block:
            secs = dict(sync=block.sync, tensor=block.tensor, vector=block.vector,
                        scalar=block.scalar, gpsimd=block.gpsimd)
            i, n = 0, len(self.ops)
            while i < n:
                eng = self.ops[i][0]
                j = i
                while j < n and self.ops[j][0] == eng:
                    j += 1
                chunk = self.ops[i:j]

                def body(e, chunk=chunk):
                    for _, fn in chunk:
                        fn(e)
                secs[eng](body)
                i = j



NWAR = 4


class MK:
    def __init__(self, parts, core_id=0):
        self.parts = parts
        self.core_id = core_id
        nc = bass.Bass("TRN2", target_bir_lowering=False)
        self.nc = nc
        self._es = []
        dt = nc.dram_tensor
        need = set()
        for p in parts:
            need.add(p[:6] if p.startswith("l0attn") or p.startswith("l1attn")
                     else p[:5])
        self.need = need
        self.xin = dt("xin", [TOK, D], F32, kind="ExternalInput").ap()
        self.y = dt("y", [TOK, D], F32, kind="ExternalOutput").ap()
        self.resid = dt("resid", [TOK, D], F32, kind="Internal").ap()
        if "l0ffn" in need:
            self.Wp = dt("Wp", [8, 4, 128, 4096], F32, kind="ExternalInput").ap()
            self.AallTp = dt("AallTp", [128, 4096], F32, kind="ExternalInput").ap()
            self.BallT = dt("BallT", [128, 4096], F32, kind="ExternalInput").ap()
            self.brow = dt("brow", [1, 4096], F32, kind="ExternalInput").ap()
            self.rWTp = dt("rWTp", [128, 256], F32, kind="ExternalInput").ap()
            self.rb2 = dt("rb2", [2, 8], F32, kind="ExternalInput").ap()
            self.rmask = dt("rmask", [128, NT * 2], F32, kind="ExternalInput").ap()
            self.bselc = dt("bselc", [2, 1], F32, kind="ExternalInput").ap()
            self.rsum_in = dt("rsum_in", [128, 64], F32, kind="Internal").ap()
            self.rsum_out = dt("rsum_out", [128, 64], F32, kind="Internal").ap()
        if "l0attn" in need:
            self.Wqp = dt("Wqp", [8, 4, 128, 4096], F32, kind="ExternalInput").ap()
            self.Wop = dt("Wop", [8, 4, 128, 4096], F32, kind="ExternalInput").ap()
            self.wkvp = dt("wkvp", [128, 8192], F32, kind="ExternalInput").ap()
            self.wqh = dt("wqh", [128, 16384], F32, kind="ExternalInput").ap()
            self.x2loc = dt("x2loc", [TOK, D], BF16, kind="Internal").ap()
            self.xg = dt("xg", [NCHIP * TOK, D], BF16, kind="Internal").ap()
            self.aod = dt("aod", [NCHIP * TOK, 512], BF16, kind="Internal").ap()
            self.aog = dt("aog", [NCHIP * NCHIP * TOK, 512], BF16, kind="Internal").ap()
            self.aoff = dt("aoff", [128, 64], U32, kind="ExternalInput").ap()
            self.masksc = dt("masks", [4, 128, 512], BF16, kind="ExternalInput").ap()
            self.kvloc = dt("kvloc", [TOK, 256], BF16, kind="Internal").ap()
            self.kvg = dt("kvg", [NCHIP * TOK, 256], BF16, kind="Internal").ap()

        if "l1ffn" in need:
            self.AwTp = dt("AwTp", [128, 16384], F32, kind="ExternalInput").ap()
            self.BwT = dt("BwT", [512, 4096], F32, kind="ExternalInput").ap()
            self.WqTp = dt("WqTp", [128, 512], F32, kind="ExternalInput").ap()
            self.keysT = dt("keysT", [16, 64], F32, kind="ExternalInput").ap()
            self.E64d = dt("E64", [64, 512], BF16, kind="ExternalInput").ap()
        if "l1attn" in need:
            self.iotavv = dt("iotavv", [128, 1], F32, kind="ExternalInput").ap()
            self.COSp = dt("COSp", [NKC, 128, 4096], BF16, kind="ExternalInput").ap()
            self.SINp = dt("SINp", [NKC, 128, 4096], BF16, kind="ExternalInput").ap()
            self.ICOSTp = dt("ICOSTp", [32, 128, KF], BF16, kind="ExternalInput").ap()
            self.vv = dt("vv", [TOK, 8192], BF16, kind="Internal").ap()
        self.identd = dt("ident", [128, 128], BF16, kind="ExternalInput").ap()
        self.onescold = dt("onescol", [1, 128], BF16, kind="ExternalInput").ap()
        self.E8d = dt("E8", [8, 128], BF16, kind="ExternalInput").ap()

        sb = nc.alloc_sbuf_tensor
        self.xf = [sb(f"xf{i}", [128, D], F32).ap() for i in range(2)]
        _xf0_addr = nc.lookup_mloc(self.xf[0].tensor).addr
        self.xgb = [nc.alloc_sbuf_tensor_at(
            f"xgb{i}", [128, D], BF16, offset=_xf0_addr + i * 8192).ap()
            for i in range(2)]
        self.lnb = sb("lnb", [128, D], BF16).ap()
        self.xT = sb("xT", [128, 32 * 1024], BF16).ap()
        self.evr = sb("evr", [128, 2048], F32).ap()
        self.bb = sb("bb", [128, D], BF16).ap()
        self.batt = sb("batt", [128, 20480], BF16).ap()
        self.warena = [sb(f"war{i}", [128, 4096], BF16).ap()
                       for i in range(NWAR)]
        self.stat = sb("stat", [128, 192], F32).ap()
        self.zu = sb("zu", [128, 2048], BF16).ap()
        self.zT = self.zu[:, 0:1024]
        self.zgT = self.zu[:, 1024:2048]
        self.smasks = self.zu[:, 0:2048]
        self.scr2k = sb("scr2k", [128, 1024], BF16).ap()
        self.small1 = self.scr2k[:, 0:512]
        self.small2 = self.scr2k[:, 512:1024]
        self.cbuf = self.scr2k[:, 0:1024]
        self.sident = sb("sident", [128, 128], BF16).ap()
        self.sones = sb("sones", [1, 128], BF16).ap()
        self.sE8 = sb("sE8", [8, 128], BF16).ap()
        self.sE64 = self.batt[0:64, 16384:16896]
        self.sWqTp = self.batt[:, 16896:17408]
        self.skeysT = self.batt[0:16, 17408:17472]
        self.srWTp = self.batt[:, 8192:8448]
        self.kvb = self.batt[:, 0:256]
        self.siotavv = sb("siotavv", [128, 1], F32).ap()
        self.srb2 = sb("srb2", [2, 8], F32).ap()
        self.sq2 = sb("sq2", [128, 64], F32).ap()
        self.sbsel = sb("sbsel", [2, 1], F32).ap()
        self.saoff = sb("saoff", [128, 64], U32).ap()
        self.sbselb = sb("sbselb", [2, 1], BF16).ap()
        self.sq2b = self.batt[:, 8448:8512]
        self.sg8 = sb("sg8", [128, 8], F32).ap()
        self.sidxf = sb("sidxf", [128, 8], F32).ap()
        self.swm = sb("swm", [128, 8], F32).ap()
        self.sgidx = sb("sgidx", [128, 8], U32).ap()
        self.l1s = sb("l1s", [128, 64], F32).ap()
        self.l1c = sb("l1c", [128, 64], F32).ap()
        self.l1g = sb("l1g", [128, 64], F32).ap()
        self.l1gb = sb("l1gb", [128, 64], BF16).ap()
        self.sgT = sb("sgT", [64, 128], BF16).ap()
        self.qrb = sb("qrb", [128, 16], BF16).ap()
        self.qrT = sb("qrT", [16, 128], BF16).ap()
        self.sbrow = self.bb[0:1, :]

        ps = nc.alloc_psum_tensor
        self.psc = ps("psc", [128, 2048], F32).ap()
        self.pf45 = [ps(f"pf{i}", [128, 512], F32).ap() for i in range(2)]
        self.pb = [ps(f"pbk{i}", [128, 1024], BF16).ap() for i in range(2)]

        self.r = Rec(nc)
        for s in SEMS:
            self.r.semh[s] = self._enter(nc.semaphore(s))

        self.pb_free = [0, 0]
        self.pf_free_cs = [0] * 6
        self.pf_free_cv = [0] * 6
        self.xf_free_cs = [0, 0]
        self.xf_free_cv = [0, 0]
        self.war_free = [0] * NWAR
        self.half_free = [0] * 8
        self.evr_free = [0, 0, 0, 0]
        self.lnb_free = 0
        self.xT_ready = 0
        self.evr_rr = 0
        self.ln_nm = [0, 0]
        self.ln_vv = [0, 0]
        self.tile_acc = {}
        self.xf_extra = {}
        self.lnb_gate = []

    def pop_tile_barriers(self):
        out = {}
        for t, lst in self.tile_acc.items():
            agg = {}
            for sem, val in lst:
                agg[sem] = max(agg.get(sem, 0), val)
            out[t] = sorted(agg.items())
        self.tile_acc = {}
        return out

    def pfv(self, i):
        if i < 4:
            return self.psc[:, i * 512:(i + 1) * 512]
        return self.pf45[i - 4][:]

    def _enter(self, cm):
        v = cm.__enter__()
        self._es.append(cm)
        return v

    def close(self):
        for cm in reversed(self._es):
            cm.__exit__(None, None, None)

    def emit_const_loads(self):
        r = self.r
        pairs = [(self.sident, self.identd), (self.sones, self.onescold),
                 (self.sE8, self.E8d)]
        if "l0ffn" in self.need:
            pairs += [(self.srb2, self.rb2)]
        if "l1attn" in self.need:
            pairs += [(self.siotavv, self.iotavv)]

        for dst, src in pairs:
            r.inc("sync", lambda e, d=dst, s=src: e.dma_start(out=d[:], in_=s),
                  "cq", 16)
        self.const_qx = r.cnt["cq"]

    def emit_resid_init(self):
        r = self.r
        for t in range(NT):
            r.inc("sync", lambda e, t=t: e.dma_start(
                out=self.resid[t * 128:(t + 1) * 128, :],
                in_=self.xin[t * 128:(t + 1) * 128, :]), "hr", 16)
        self.resid_init_hx = r.cnt["hr"]

    def emit_ln_a(self, t, src, barrier_ac):
        r = self.r
        tb = t % 2
        st = self.stat
        if barrier_ac:
            for sem, val in barrier_ac.get(t, []):
                r.wait("sync", sem, val)
        r.wait("sync", "cs", self.xf_free_cs[tb])
        r.wait("sync", "cv", self.xf_free_cv[tb])
        for sem, val in self.xf_extra.get(tb, []):
            r.wait("sync", sem, val)
        qsem = "qx0" if tb == 0 else "qx1"
        qv = r.inc("sync", lambda e, tb=tb, t=t, src=src: e.dma_start(
            out=self.xf[tb][:], in_=src[t * 128:(t + 1) * 128, :]), qsem, 16)
        bnb = tb * 48
        sc = 96 + tb * 24
        r.wait("vector", qsem, qv)
        for c in range(8):
            r.do("vector", lambda e, c=c, tb=tb, bnb=bnb: e.bn_stats(
                out=st[:, bnb + c * 6:bnb + (c + 1) * 6],
                in_=self.xf[tb][:, c * 512:(c + 1) * 512]))
        means = ap2(st, bnb + 1, [[3, 16]])
        vns = ap2(st, bnb + 2, [[3, 16]])
        r.selfwait("vector")
        r.do("vector", lambda e, means=means, sc=sc: e.reduce_sum(
            st[:, sc:sc + 1], means, axis=AXX))
        r.do("vector", lambda e, vns=vns, sc=sc: e.reduce_sum(
            st[:, sc + 1:sc + 2], vns, axis=AXX))
        r.do("vector", lambda e, means=means, sc=sc: e.tensor_tensor(
            out=st[:, sc + 8:sc + 24], in0=means, in1=means, op=ALU.mult))
        r.selfwait("vector")
        r.do("vector", lambda e, sc=sc: e.reduce_sum(
            st[:, sc + 2:sc + 3], st[:, sc + 8:sc + 24], axis=AXX))
        r.do("vector", lambda e, sc=sc: e.tensor_scalar(
            out=st[:, sc + 3:sc + 4], in0=st[:, sc:sc + 1], scalar1=1.0 / 16,
            scalar2=None, op0=ALU.mult))
        r.selfwait("vector")
        r.do("vector", lambda e, sc=sc: e.tensor_scalar(
            out=st[:, sc + 4:sc + 5], in0=st[:, sc + 3:sc + 4],
            scalar1=st[:, sc + 3:sc + 4], scalar2=-1.0,
            op0=ALU.mult, op1=ALU.mult))
        r.selfwait("vector")
        r.do("vector", lambda e, sc=sc: e.scalar_tensor_tensor(
            out=st[:, sc + 5:sc + 6], in0=st[:, sc + 1:sc + 2], scalar=1.0 / D,
            in1=st[:, sc + 4:sc + 5], op0=ALU.mult, op1=ALU.add))
        r.selfwait("vector")
        r.do("vector", lambda e, sc=sc: e.scalar_tensor_tensor(
            out=st[:, sc + 6:sc + 7], in0=st[:, sc + 2:sc + 3], scalar=1.0 / 16,
            in1=st[:, sc + 5:sc + 6], op0=ALU.mult, op1=ALU.add))
        r.selfwait("vector")
        vv_ = r.inc("vector", lambda e, sc=sc: e.tensor_scalar(
            out=st[:, sc + 6:sc + 7], in0=st[:, sc + 6:sc + 7], scalar1=EPS,
            scalar2=None, op0=ALU.add), "cv")
        self.ln_vv[tb] = vv_

    def emit_ln_a2(self, t):
        r = self.r
        tb = t % 2
        st = self.stat
        sc = 96 + tb * 24
        r.wait("scalar", "cv", self.ln_vv[tb])
        sv = r.inc("scalar", lambda e, sc=sc: e.activation(
            out=st[:, sc + 7:sc + 8], in_=st[:, sc + 6:sc + 7], func=AF.Sqrt), "cs")
        r.wait("vector", "cs", sv)
        r.do("vector", lambda e, sc=sc: e.reciprocal(
            out=st[:, sc + 20:sc + 21], in_=st[:, sc + 7:sc + 8]))
        r.selfwait("vector")
        nm = r.inc("vector", lambda e, sc=sc: e.tensor_scalar(
            out=st[:, sc + 21:sc + 22], in0=st[:, sc + 3:sc + 4],
            scalar1=st[:, sc + 20:sc + 21], scalar2=-1.0,
            op0=ALU.mult, op1=ALU.mult), "cv")
        self.xf_free_cv[tb] = nm
        self.ln_nm[tb] = nm

    def emit_ln_b(self, t):
        r = self.r
        tb = t % 2
        st = self.stat
        sc = 96 + tb * 24
        r.wait("scalar", "cv", self.ln_nm[tb])
        r.wait("scalar", "ct", self.lnb_free)
        for sem, val in self.lnb_gate:
            r.wait("scalar", sem, val)
        self.lnb_gate = []
        if getattr(self, "lnb_store_g1", 0):
            r.wait("scalar", "g1", self.lnb_store_g1)
        ap = r.inc("scalar", lambda e, tb=tb, sc=sc: e.activation(
            out=self.lnb[:], in_=self.xf[tb][:], func=AF.Identity,
            bias=st[:, sc + 21:sc + 22], scale=st[:, sc + 20:sc + 21]), "cs")
        self.xf_free_cs[tb] = ap
        r.wait("tensor", "cq", r.cnt["cq"])
        for g in range(4):
            bi = g % 2
            r.wait("tensor", "cs", ap)
            r.wait("tensor", "cv", self.pb_free[bi])
            for cc in range(7):
                c = g * 8 + cc
                r.do("tensor", lambda e, bi=bi, cc=cc, c=c: e.transpose(
                    self.pb[bi][:, cc * 128:(cc + 1) * 128],
                    self.lnb[:, c * 128:(c + 1) * 128], self.sident[:]))
            tv = r.inc("tensor", lambda e, bi=bi, g=g: e.transpose(
                self.pb[bi][:, 7 * 128:8 * 128],
                self.lnb[:, (g * 8 + 7) * 128:(g * 8 + 8) * 128],
                self.sident[:]), "ct")
            if g == 3:
                self.lnb_free = tv
            r.wait("vector", "ct", tv)

            def ev(e, bi=bi, g=g, t=t):
                dst = ap2(self.xT, g * 8 * 1024 + t * 128, [[1024, 8], [1, 128]])
                src_ = ap2(self.pb[bi], 0, [[128, 8], [1, 128]])
                return e.tensor_copy(out=dst, in_=src_)
            self.pb_free[bi] = r.inc("vector", ev, "cv")
        self.xT_ready = self.pb_free[1]
        return ap

    def ln_pipeline(self, src, barrier_ac):
        self.emit_ln_a(0, src, barrier_ac)
        self.emit_ln_a2(0)
        if NT > 1:
            self.emit_ln_a(1, src, barrier_ac)
        for t in range(NT):
            ap = self.emit_ln_b(t)
            if t + 1 < NT:
                self.emit_ln_a2(t + 1)
            if t + 2 < NT:
                self.emit_ln_a(t + 2, src, barrier_ac)
            yield t, ap

    def emit_epilogue(self, t, o, bank, scale, mm_ct, resid_hx=None):
        r = self.r
        ei = self.evr_rr % 4
        self.evr_rr += 1
        r.wait("scalar", "ct", mm_ct)
        r.wait("scalar", f"ac{ei}", self.evr_free[ei])
        ev = r.inc("scalar", lambda e, ei=ei, bank=bank, scale=scale: e.activation(
            out=self.evr[:, ei * 512:(ei + 1) * 512], in_=self.pfv(bank),
            func=AF.Identity, scale=float(scale)), "cs")
        self.pf_free_cs[bank] = ev
        r.wait("gpsimd", "cs", ev)
        if resid_hx:
            r.wait("gpsimd", "hr", resid_hx)
        av = r.inc("gpsimd", lambda e, t=t, o=o, ei=ei: e.dma_start(
            out=self.resid[t * 128:(t + 1) * 128, o * 512:(o + 1) * 512],
            in_=self.evr[:, ei * 512:(ei + 1) * 512], accum_op=ALU.add),
            f"ac{ei}", 16)
        self.evr_free[ei] = av
        self.tile_acc.setdefault(t, []).append((f"ac{ei}", av))
        return av

    def load_panel(self, panel_src, o, q):
        r = self.r
        si = (o * 4 + q) % NWAR
        r.wait("gpsimd", "ct", max(self.war_free[si], self.half_free[2 * si],
                                   self.half_free[2 * si + 1]))
        return (f"wl{si}", r.inc(
            "gpsimd", lambda e, o=o, q=q, si=si, p=panel_src:
            e.dma_start(out=self.warena[si][:], in_=p[o, q]), f"wl{si}", 16))

    def load_half(self, panel_src, o, q, h):
        r = self.r
        ls = (o * 8 + q * 2 + h) % 8
        si, half = ls // 2, ls % 2
        r.wait("gpsimd", "ct", max(self.half_free[ls], self.war_free[si]))
        return (f"wl{ls}", r.inc(
            "gpsimd", lambda e, o=o, q=q, h=h, si=si, half=half, p=panel_src:
            e.dma_start(out=self.warena[si][:, half * 2048:(half + 1) * 2048],
                        in_=p[o, q][:, h * 2048:(h + 1) * 2048]), f"wl{ls}", 16))

    def emit_l0ffn(self, src, barrier_ac, resid_hx):
        r = self.r
        r.wait("gpsimd", "ct", r.cnt["ct"])
        aal = r.inc("gpsimd", lambda e: e.dma_start(
            out=self.batt[:, 0:4096], in_=self.AallTp), "wlb", 16)
        bal = r.inc("gpsimd", lambda e: e.dma_start(
            out=self.batt[:, 4096:8192], in_=self.BallT), "wlb", 16)
        rmv = r.inc("gpsimd", lambda e: e.dma_start(
            out=self.small2[:, 0:16], in_=self.rmask), "wlb", 16)
        bsv = r.inc("gpsimd", lambda e: e.dma_start(
            out=self.sbsel[:], in_=self.bselc), "wlb", 16)
        brw = r.inc("gpsimd", lambda e: e.dma_start(
            out=self.sbrow, in_=self.brow), "wlb", 16)
        rwv = r.inc("gpsimd", lambda e: e.dma_start(
            out=self.srWTp, in_=self.rWTp), "wlb", 16)
        wlb_all = r.cnt["wlb"]
        for t, ap in self.ln_pipeline(src, barrier_ac):
            r.wait("tensor", "wlb", wlb_all)
            if t == 0:
                r.wait("tensor", "cv", self.pf_free_cv[5])
            for c in range(32):
                fn = lambda e, c=c, t=t: e.matmul(
                    self.pfv(5)[:, c * 2:c * 2 + 2],
                    self.lnb[:, c * 128:(c + 1) * 128],
                    self.small2[:, t * 2:t * 2 + 2],
                    start=(t == 0 and c == 0), stop=(t == 7 and c == 31))
                if t == 7 and c == 31:
                    self.router_ct = r.inc("tensor", fn, "ct")
                else:
                    r.do("tensor", fn)
            r.wait("tensor", "cv", self.xT_ready)
            r.wait("tensor", "wlb", wlb_all)
            r.wait("tensor", "cs", self.pf_free_cs[0])
            for c in range(31):
                r.do("tensor", lambda e, c=c, t=t: e.matmul(
                    self.pfv(0)[:, 0:128],
                    self.xT[:, c * 1024 + t * 128:c * 1024 + (t + 1) * 128],
                    self.batt[:, c * 128:(c + 1) * 128],
                    start=(c == 0), stop=False))
            zt = r.inc("tensor", lambda e, t=t: e.matmul(
                self.pfv(0)[:, 0:128],
                self.xT[:, 31 * 1024 + t * 128:31 * 1024 + (t + 1) * 128],
                self.batt[:, 31 * 128:32 * 128],
                start=False, stop=True), "ct")
            self.lnb_free = zt
            r.wait("scalar", "ct", zt)
            ze = r.inc("scalar", lambda e: e.activation(
                out=self.small1[:, 0:128], in_=self.pfv(0)[:, 0:128],
                func=AF.Identity), "cs")
            self.pf_free_cs[0] = ze
            r.wait("tensor", "cs", ze)
            r.wait("tensor", "cv", self.pb_free[0])
            ztr = r.inc("tensor", lambda e: e.transpose(
                self.pb[0][:, 0:128], self.small1[:, 0:128], self.sident[:]),
                "ct")
            r.wait("vector", "ct", ztr)
            self.pb_free[0] = r.inc("vector", lambda e, t=t: e.tensor_copy(
                out=self.zT[:, t * 128:(t + 1) * 128], in_=self.pb[0][:, 0:128]),
                "cv")
        st = self.stat
        r.wait("vector", "ct", self.router_ct)
        rs = r.inc("vector", lambda e: e.tensor_scalar(
            out=self.sq2[:], in0=self.pfv(5)[:, 0:64], scalar1=1.0 / MEAN_CNT,
            scalar2=None, op0=ALU.mult), "cv")
        self.pf_free_cv[5] = rs
        r.wait("sync", "cv", rs)
        s1 = r.inc("sync", lambda e: e.dma_start(out=self.rsum_in, in_=self.sq2[:]),
                   "hs", 16)
        r.wait("gpsimd", "hs", s1)
        ccv = r.inc("gpsimd", lambda e: e.collective_compute(
            "AllReduce", ALU.add, replica_groups=[list(range(NCHIP))],
            ins=[self.rsum_in.opt()], outs=[self.rsum_out.opt()]), "cc", 1)
        r.wait("sync", "cc", ccv)
        s2 = r.inc("sync", lambda e: e.dma_start(out=self.sq2[:], in_=self.rsum_out),
                   "qxr", 16)
        r.wait("vector", "qxr", s2)
        qb = r.inc("vector", lambda e: e.tensor_copy(
            out=self.sq2b[:], in_=self.sq2[:]), "cv")
        r.wait("tensor", "cv", qb)
        r.wait("tensor", "cq", self.const_qx)
        r.wait("tensor", "wlb", wlb_all)
        lg = None
        for c in range(32):
            fn = lambda e, c=c: e.matmul(
                self.pfv(1)[0:2, 0:8],
                self.sq2b[:, c * 2:(c + 1) * 2],
                self.srWTp[:, c * 8:(c + 1) * 8],
                start=(c == 0), stop=(c == 31))
            if c == 31:
                lg = r.inc("tensor", fn, "ct")
            else:
                r.do("tensor", fn)
        r.wait("vector", "ct", lg)
        r.do("vector", lambda e: e.tensor_tensor(
            out=st[0:2, 144:152], in0=self.pfv(1)[0:2, 0:8], in1=self.srb2[:],
            op=ALU.add))
        r.selfwait("vector")
        r.do("vector", lambda e: e.reduce_max(
            st[0:2, 152:153], st[0:2, 144:152], axis=AXX))
        r.selfwait("vector")
        ng = r.inc("vector", lambda e: e.tensor_scalar(
            out=st[0:2, 153:154], in0=st[0:2, 152:153], scalar1=-1.0,
            scalar2=None, op0=ALU.mult), "cv")
        self.pf_free_cv[1] = ng
        r.wait("scalar", "cv", ng)
        ex = r.inc("scalar", lambda e: e.activation(
            out=st[0:2, 160:168], in_=st[0:2, 144:152],
            func=AF.Exp, bias=st[0:2, 153:154], accum_out=st[0:2, 168:169]), "cs")
        r.wait("vector", "cs", ex)
        r.do("vector", lambda e: e.reciprocal(
            out=st[0:2, 169:170], in_=st[0:2, 168:169]))
        r.selfwait("vector")
        r.do("vector", lambda e: e.tensor_scalar(
            out=st[0:2, 144:152], in0=st[0:2, 160:168], scalar1=st[0:2, 169:170],
            scalar2=None, op0=ALU.mult))
        r.selfwait("vector")
        rb = r.inc("vector", lambda e: e.tensor_copy(
            out=self.l1gb[0:2, 0:8], in_=st[0:2, 144:152]), "cv")
        r.selfwait("vector")
        rb2 = r.inc("vector", lambda e: e.tensor_copy(
            out=self.sbselb[:], in_=self.sbsel[:]), "cv")
        r.wait("tensor", "cv", rb)
        r.wait("tensor", "cv", rb2)
        r.wait("tensor", "cv", self.pf_free_cv[1])
        rs_ct = r.inc("tensor", lambda e: e.matmul(
            self.pfv(1)[0:8, 0:1], self.l1gb[0:2, 0:8], self.sbselb[:],
            start=True, stop=True), "ct")
        r.wait("vector", "ct", rs_ct)
        rse = r.inc("vector", lambda e: e.tensor_copy(
            out=self.sgT[0:8, 0:1], in_=self.pfv(1)[0:8, 0:1]), "cv")
        r.wait("tensor", "cv", rse)
        gv = r.inc("tensor", lambda e: e.matmul(
            self.pfv(1)[:, 0:1], self.sE8[:], self.sgT[0:8, 0:1],
            start=True, stop=True), "ct")
        r.wait("vector", "ct", gv)
        gve = r.inc("vector", lambda e: e.tensor_copy(
            out=st[:, 170:171], in_=self.pfv(1)[:, 0:1]), "cv")
        self.pf_free_cv[1] = gve
        r.selfwait("vector")
        gz = 0
        for t in range(NT):
            gz = r.inc("vector", lambda e, t=t: e.tensor_scalar(
                out=self.zgT[:, t * 128:(t + 1) * 128],
                in0=self.zT[:, t * 128:(t + 1) * 128],
                scalar1=st[:, 170:171], scalar2=None,
                op0=ALU.mult), "cv")
        loaded = {}
        for q in range(4):
            for h in range(2):
                loaded[(0, q, h)] = self.load_half(self.Wp, 0, q, h)
        r.wait("tensor", "cv", self.xT_ready)
        r.wait("tensor", "wlb", wlb_all)
        lora_waited = False
        for o in range(8):
            for grp in range(2):
                tiles = list(range(grp * 4, grp * 4 + 4))
                for c in range(32):
                    q = c // 8
                    h = (c % 8) // 4
                    si = (o * 4 + q) % NWAR
                    if c % 4 == 0:
                        r.wait("tensor", *loaded[(o, q, h)])
                    for ti, t in enumerate(tiles):
                        bank = t % 4
                        if c == 0:
                            r.wait("tensor", "cs", self.pf_free_cs[bank])
                            r.wait("tensor", "cv", self.pf_free_cv[bank])
                        fn = lambda e, c=c, t=t, si=si, bank=bank: e.matmul(
                            self.pfv(bank),
                            self.xT[:, c * 1024 + t * 128:c * 1024 + (t + 1) * 128],
                            self.warena[si][:, (c % 8) * 512:((c % 8) + 1) * 512],
                            start=(c == 0), stop=False)
                        if grp == 1 and c % 4 == 3 and ti == 3:
                            ls = (o * 8 + q * 2 + h) % 8
                            self.half_free[ls] = r.inc("tensor", fn, "ct")
                            if o + 1 < 8:
                                loaded[(o + 1, q, h)] = self.load_half(self.Wp, o + 1, q, h)
                        else:
                            r.do("tensor", fn)
                if not lora_waited:
                    r.wait("tensor", "cv", gz)
                    lora_waited = True
                for t in tiles:
                    bank = t % 4
                    r.do("tensor", lambda e, t=t, o=o, bank=bank: e.matmul(
                        self.pfv(bank), self.sones[0:1, :],
                        self.sbrow[0:1, o * 512:(o + 1) * 512],
                        start=False, stop=False))
                    mv = r.inc("tensor", lambda e, t=t, o=o, bank=bank: e.matmul(
                        self.pfv(bank), self.zgT[:, t * 128:(t + 1) * 128],
                        self.batt[:, 4096 + o * 512:4096 + (o + 1) * 512],
                        start=False, stop=True), "ct")
                    self.emit_epilogue(t, o, bank, 0.5, mv, resid_hx)
        return self.pop_tile_barriers()



    def emit_l0attn(self, src, barrier_ac, resid_hx):
        r = self.r
        st = self.stat
        SC = 0.0883883476483184
        r.wait("gpsimd", "ct", r.cnt["ct"])
        r.inc("gpsimd", lambda e: e.dma_start(
            out=self.batt[:, 0:8192], in_=self.wkvp), "wlb", 16)
        for i in range(4):
            r.wait("gpsimd", "ct", self.war_free[i])
            r.inc("gpsimd", lambda e, i=i: e.dma_start(
                out=self.warena[i][:],
                in_=self.wqh[:, i * 4096:(i + 1) * 4096]), "wlb", 16)
        r.inc("gpsimd", lambda e: e.dma_start(
            out=self.saoff[:], in_=self.aoff), "wlb", 16)
        wlb_all = r.cnt["wlb"]
        r.wait("sync", "ct", r.cnt["ct"])
        smv = r.inc("sync", lambda e: e.dma_start(
            out=ap2(self.smasks, 0, [[512, 4], [1, 512]]),
            in_=self.masksc.rearrange("m p j -> p m j")), "cq", 16)
        au0 = r.cnt["au"]
        kv_ct = 0
        for t, ap in self.ln_pipeline(src, barrier_ac):
            r.wait("sync", "cs", ap)
            sx = r.inc("sync", lambda e, t=t: e.dma_start(
                out=self.x2loc[t * 128:(t + 1) * 128, :], in_=self.lnb[:]),
                "g1", 16)
            self.lnb_store_g1 = sx
            if t % 4 == 3:
                h = t // 4
                r.wait("gpsimd", "g1", sx)
                r.inc("gpsimd", lambda e, h=h: e.collective_compute(
                    "AllGather", ALU.bypass, replica_groups=[list(range(NCHIP))],
                    ins=[self.x2loc[h * 512:(h + 1) * 512, :].opt()],
                    outs=[self.xg[h * 4096:(h + 1) * 4096, :].opt()]), "cc", 1)
            r.wait("tensor", "wlb", wlb_all)
            r.wait("tensor", "cv", self.xT_ready)
            if t == 0:
                r.wait("tensor", "cs", self.pf_free_cs[4])
            for c in range(31):
                r.do("tensor", lambda e, c=c, t=t: e.matmul(
                    self.pfv(4)[:, 0:256],
                    self.xT[:, c * 1024 + t * 128:c * 1024 + (t + 1) * 128],
                    self.batt[:, c * 256:(c + 1) * 256],
                    start=(c == 0), stop=False))
            kv_ct = r.inc("tensor", lambda e, t=t: e.matmul(
                self.pfv(4)[:, 0:256],
                self.xT[:, 31 * 1024 + t * 128:31 * 1024 + (t + 1) * 128],
                self.batt[:, 31 * 256:32 * 256],
                start=False, stop=True), "ct")
            r.wait("scalar", "ct", kv_ct)
            r.wait("scalar", "au", getattr(self, "kv_store_au", 0))
            kve = r.do("scalar", lambda e: e.activation(
                out=self.batt[:, 16640:16896], in_=self.pfv(4)[:, 0:256],
                func=AF.Identity))
            self.pf_free_cs[4] = kve
            r.wait("sync", "cs", kve)
            self.kv_store_au = r.inc("sync", lambda e, t=t: e.dma_start(
                out=self.kvloc[t * 128:(t + 1) * 128, :],
                in_=self.batt[:, 16640:16896]), "au", 16)
        r.wait("gpsimd", "au", r.cnt["au"])
        cc_all = r.inc("gpsimd", lambda e: e.collective_compute(
            "AllGather", ALU.bypass, replica_groups=[list(range(NCHIP))],
            ins=[self.kvloc.opt()], outs=[self.kvg.opt()]), "cc", 1)
        r.wait("vector", "ct", kv_ct)
        vmf = r.do("vector", lambda e: e.memset(self.batt[:, 8192:16640], 1.0))
        r.wait("sync", "cc", cc_all)
        r.wait("sync", "cv", vmf)
        for beta in range(2):
            ksem = f"kT{beta}"
            for B in range(16):
                r0 = beta * 4096 + B * 256
                r.inc("sync", lambda e, beta=beta, B=B, r0=r0: e.dma_start(
                    out=self.batt[:, beta * 4096 + B * 256:beta * 4096 + (B + 1) * 256],
                    in_=self.kvg[r0:r0 + 256, 0:128], transpose=True), ksem, 16)
                vdst = ap2(self.batt, 8192 + beta * 4224 + B * 2 * 132,
                           [[132, 2], [1, 128]])
                vsrc = self.kvg[r0:r0 + 256, 128:256].rearrange(
                    "(k p) h -> p k h", p=128)
                r.inc("sync", lambda e, vdst=vdst, vsrc=vsrc: e.dma_start(
                    out=vdst, in_=vsrc), ksem, 16)
        kt_all = [r.cnt["kT0"], r.cnt["kT1"]]
        xgb_free = [0, 0]
        xgt_cv = [0, 0]
        aod0 = r.cnt["au"]
        for qi in range(64):
            beta, i = qi // 32, qi % 32
            nkb = i // 4 + 1
            nch = i + 1
            di = i % 4
            p2 = qi % 2
            qsem = "qx0" if p2 == 0 else "qx1"
            r.wait("sync", "ct", xgb_free[p2])
            g_l = qi % 8
            xgrow = (g_l // 4) * 4096 + (qi // 8) * 512 + (g_l % 4) * 128
            qv = r.inc("sync", lambda e, xgrow=xgrow, p2=p2: e.dma_start(
                out=self.xgb[p2][:], in_=self.xg[xgrow:xgrow + 128, :]),
                qsem, 16)
            r.wait("tensor", qsem, qv)
            for g in range(4):
                bi = g % 2
                r.wait("tensor", "cv", self.pb_free[bi])
                for cc in range(7):
                    c = g * 8 + cc
                    r.do("tensor", lambda e, bi=bi, cc=cc, c=c, p2=p2: e.transpose(
                        self.pb[bi][:, cc * 128:(cc + 1) * 128],
                        self.xgb[p2][:, c * 128:(c + 1) * 128], self.sident[:]))
                tv = r.inc("tensor", lambda e, bi=bi, g=g, p2=p2: e.transpose(
                    self.pb[bi][:, 7 * 128:8 * 128],
                    self.xgb[p2][:, (g * 8 + 7) * 128:(g * 8 + 8) * 128],
                    self.sident[:]), "ct")
                if g == 3:
                    xgb_free[p2] = tv
                r.wait("vector", "ct", tv)
                self.pb_free[bi] = r.do("vector", lambda e, bi=bi, g=g, p2=p2:
                    e.tensor_copy(
                        out=self.xT[:, p2 * 4096 + g * 1024:p2 * 4096 + (g + 1) * 1024],
                        in_=self.pb[bi][:]))
            xgt_cv[p2] = self.pb_free[1]
            r.wait("tensor", "cv", xgt_cv[p2])
            r.wait("tensor", "cs", self.pf_free_cs[5])
            qp_ct = None
            for c in range(32):
                fn = lambda e, c=c, p2=p2: e.matmul(
                    self.pfv(5),
                    self.xT[:, p2 * 4096 + c * 128:p2 * 4096 + (c + 1) * 128],
                    self.warena[c // 8][:, (c % 8) * 512:((c % 8) + 1) * 512],
                    start=(c == 0), stop=(c == 31))
                if c == 31:
                    qp_ct = r.inc("tensor", fn, "ct")
                else:
                    r.do("tensor", fn)
            r.wait("scalar", "ct", qp_ct)
            qe = r.do("scalar", lambda e: e.activation(
                out=self.scr2k[:, 0:512], in_=self.pfv(5), func=AF.Identity))
            self.pf_free_cs[5] = qe
            r.wait("tensor", "cs", qe)
            r.wait("tensor", "cv", self.pb_free[0])
            qt_ct = None
            for hl in range(4):
                fn = lambda e, hl=hl: e.transpose(
                    self.pb[0][:, hl * 128:(hl + 1) * 128],
                    self.scr2k[:, hl * 128:(hl + 1) * 128], self.sident[:])
                if hl == 3:
                    qt_ct = r.inc("tensor", fn, "ct")
                else:
                    r.do("tensor", fn)
            r.wait("vector", "ct", qt_ct)
            qte = r.do("vector", lambda e: e.tensor_copy(
                out=self.scr2k[:, 512:1024], in_=self.pb[0][:, 0:512]))
            self.pb_free[0] = qte
            r.wait("tensor", "cv", qte)
            r.wait("tensor", "cq", r.cnt["cq"])
            r.wait("tensor", f"kT{beta}", kt_all[beta])
            jobs = []
            for hl in range(4):
                for pair0 in range(0, nkb, 2):
                    jobs.append((hl, pair0))

            def emit_scores(idx):
                hl, pair0 = jobs[idx]
                half = idx % 2
                kbs = [kb for kb in (pair0, pair0 + 1) if kb < nkb]
                w = 512 * len(kbs)
                sl = idx % 3
                sc_ct = None
                for ki, kb in enumerate(kbs):
                    bank = half * 2 + ki
                    r.wait("tensor", "cs", self.pf_free_cs[bank])
                    diag = (kb == nkb - 1)
                    fn = lambda e, kb=kb, hl=hl, beta=beta, bank=bank, diag=diag: \
                        e.matmul(
                            self.pfv(bank),
                            self.scr2k[:, 512 + hl * 128:512 + (hl + 1) * 128],
                            self.batt[:, beta * 4096 + kb * 512:beta * 4096 + (kb + 1) * 512],
                            start=True, stop=not diag)
                    if diag:
                        r.do("tensor", fn)
                        fnm = lambda e, bank=bank, di=di: e.matmul(
                            self.pfv(bank), self.sident[:],
                            self.smasks[:, di * 512:(di + 1) * 512],
                            start=False, stop=True)
                        sc_ct = r.inc("tensor", fnm, "ct")
                    elif ki == len(kbs) - 1:
                        sc_ct = r.inc("tensor", fn, "ct")
                    else:
                        r.do("tensor", fn)
                r.wait("scalar", "ct", sc_ct)
                if idx >= 3:
                    r.wait("scalar", "ct", self.ptr_ct[idx - 3])
                elif getattr(self, "prev_qi_ptr", 0):
                    r.wait("scalar", "ct", self.prev_qi_ptr)
                epx = r.do("scalar", lambda e, half=half, sl=sl, w=w: e.activation(
                    out=self.bb[:, sl * 1024:sl * 1024 + w],
                    in_=self.psc[:, half * 1024:half * 1024 + w],
                    func=AF.Exp, scale=SC))
                for ki in range(len(kbs)):
                    self.pf_free_cs[half * 2 + ki] = epx
                self.exp_cs[idx] = epx

            def emit_T(idx):
                hl, pair0 = jobs[idx]
                kbs = [kb for kb in (pair0, pair0 + 1) if kb < nkb]
                sl = idx % 3
                ncg = min(len(kbs) * 4, nch - pair0 * 4)
                pslot = idx % 2
                bi = idx % 2
                r.wait("tensor", "cs", self.exp_cs[idx])
                r.wait("tensor", "cv", self.pb_free[bi])
                pt_ct = None
                for j in range(ncg):
                    fn = lambda e, bi=bi, j=j, sl=sl: e.transpose(
                        self.pb[bi][:, j * 128:(j + 1) * 128],
                        self.bb[:, sl * 1024 + j * 128:sl * 1024 + (j + 1) * 128],
                        self.sident[:])
                    if j == ncg - 1:
                        pt_ct = r.inc("tensor", fn, "ct")
                    else:
                        r.do("tensor", fn)
                self.ptr_ct[idx] = pt_ct
                r.wait("vector", "ct", pt_ct)
                pte = r.do("vector", lambda e, bi=bi, ncg=ncg, pslot=pslot: e.tensor_copy(
                    out=self.xT[:, 8192 + pslot * 1024:8192 + pslot * 1024 + ncg * 128],
                    in_=self.pb[bi][:, 0:ncg * 128]))
                self.pb_free[bi] = pte
                self.pte_cv[idx] = pte

            def emit_pv(idx):
                hl, pair0 = jobs[idx]
                kbs = [kb for kb in (pair0, pair0 + 1) if kb < nkb]
                ncg = min(len(kbs) * 4, nch - pair0 * 4)
                pslot = idx % 2
                pvbank = 4 + hl % 2
                r.wait("tensor", "cv", self.pte_cv[idx])
                if pair0 == 0:
                    r.wait("tensor", "cs", self.pf_free_cs[pvbank])
                    r.wait("tensor", "cv", self.pf_free_cv[pvbank])
                pv_ct = None
                for j in range(ncg):
                    c = pair0 * 4 + j
                    fn = lambda e, j=j, c=c, beta=beta, nch=nch, pslot=pslot, pvbank=pvbank: \
                        e.matmul(
                            self.pfv(pvbank)[:, 0:132],
                            self.xT[:, 8192 + pslot * 1024 + j * 128:8192 + pslot * 1024 + (j + 1) * 128],
                            self.batt[:, 8192 + beta * 4224 + c * 132:8192 + beta * 4224 + (c + 1) * 132],
                            start=(c == 0), stop=(c == nch - 1))
                    if c == nch - 1:
                        pv_ct = r.inc("tensor", fn, "ct")
                    else:
                        r.do("tensor", fn)
                if pv_ct is not None:
                    r.wait("vector", "ct", pv_ct)
                    dn = r.do("vector", lambda e, pvbank=pvbank: e.reciprocal(
                        out=st[:, 176 + hl:177 + hl], in_=self.pfv(pvbank)[:, 128:129]))
                    r.wait("scalar", "cv", dn)
                    if hl == 0:
                        r.wait("scalar", "g0", getattr(self, "aod_g0", 0))
                    ave = r.do("scalar", lambda e, hl=hl, pvbank=pvbank: e.activation(
                        out=self.lnb[:, hl * 128:(hl + 1) * 128],
                        in_=self.pfv(pvbank)[:, 0:128], func=AF.Identity,
                        scale=st[:, 176 + hl:177 + hl]))
                    self.pf_free_cs[pvbank] = ave
                    self.pf_free_cv[pvbank] = dn
                    self.aob_cs[hl] = ave

            self.ptr_ct = {}
            self.exp_cs = {}
            self.aob_cs = {}
            self.pte_cv = {}
            nj = len(jobs)
            for idx in range(nj):
                emit_scores(idx)
                if idx >= 1:
                    emit_T(idx - 1)
                if idx >= 2:
                    emit_pv(idx - 2)
            emit_T(nj - 1)
            if nj >= 2:
                emit_pv(nj - 2)
            emit_pv(nj - 1)
            self.prev_qi_ptr = self.ptr_ct[nj - 1]
            r.wait("gpsimd", "cs", max(self.aob_cs.values()))
            self.aod_g0 = r.inc("gpsimd", lambda e, qi=qi: e.dma_start(
                out=self.aod[qi * 128:(qi + 1) * 128, :], in_=self.lnb[:, 0:512]),
                "g0", 16)
            if qi % 32 == 31:
                g = qi // 32
                r.wait("gpsimd", "g0", self.aod_g0)
                r.inc("gpsimd", lambda e, g=g: e.collective_compute(
                    "AllGather", ALU.bypass, replica_groups=[list(range(NCHIP))],
                    ins=[self.aod[g * 4096:(g + 1) * 4096, :].opt()],
                    outs=[self.aog[g * 32768:(g + 1) * 32768, :].opt()]), "cc", 1)
        aoe_free = 0
        loaded = {}
        for i in range(4):
            self.war_free[i] = r.cnt["ct"]
        for i in range(8):
            self.half_free[i] = r.cnt["ct"]
        for q in range(4):
            for h in range(2):
                loaded[(0, q, h)] = self.load_half(self.Wop, 0, q, h)
        aoT_cv = [0] * NT
        r.wait("gpsimd", "cc", r.cnt["cc"])
        for t in range(NT):
            r.wait("gpsimd", "ct", aoe_free)
            for eidx in range(8):
                av = r.inc("gpsimd", lambda e, t=t, eidx=eidx: e.indirect_dma_start(
                    out=self.lnb[:, eidx * 512:(eidx + 1) * 512], out_offset=None,
                    in_=self.aog,
                    in_offset=IndirectOffsetOnAxis(
                        self.saoff[:, t * 8 + eidx:t * 8 + eidx + 1], 1)),
                    "qt", 16)
            r.wait("tensor", "qt", av)
            for g in range(4):
                bi = g % 2
                r.wait("tensor", "cv", self.pb_free[bi])
                for cc in range(7):
                    c = g * 8 + cc
                    r.do("tensor", lambda e, bi=bi, cc=cc, c=c: e.transpose(
                        self.pb[bi][:, cc * 128:(cc + 1) * 128],
                        self.lnb[:, c * 128:(c + 1) * 128], self.sident[:]))
                tv = r.inc("tensor", lambda e, bi=bi, g=g: e.transpose(
                    self.pb[bi][:, 7 * 128:8 * 128],
                    self.lnb[:, (g * 8 + 7) * 128:(g * 8 + 8) * 128],
                    self.sident[:]), "ct")
                if g == 3:
                    aoe_free = tv
                r.wait("vector", "ct", tv)

                def ev(e, bi=bi, g=g, t=t):
                    dst = ap2(self.xT, g * 8 * 1024 + t * 128, [[1024, 8], [1, 128]])
                    src_ = ap2(self.pb[bi], 0, [[128, 8], [1, 128]])
                    return e.tensor_copy(out=dst, in_=src_)
                self.pb_free[bi] = r.do("vector", ev)
            aoT_cv[t] = self.pb_free[1]
        for o in range(8):
            mvs = {}
            for grp in range(2):
                tiles = list(range(grp * 4, grp * 4 + 4))
                for c in range(32):
                    q = c // 8
                    h = (c % 8) // 4
                    si = (o * 4 + q) % NWAR
                    if c % 4 == 0:
                        r.wait("tensor", *loaded[(o, q, h)])
                    for ti, t in enumerate(tiles):
                        bank = t % 4
                        if c == 0:
                            r.wait("tensor", "cv", aoT_cv[t])
                            r.wait("tensor", "cs", self.pf_free_cs[bank])
                            r.wait("tensor", "cv", self.pf_free_cv[bank])
                        last_mm = (c == 31)
                        hb = (grp == 1 and c % 4 == 3 and ti == 3)
                        fn = lambda e, c=c, t=t, si=si, bank=bank: e.matmul(
                            self.pfv(bank),
                            self.xT[:, c * 1024 + t * 128:c * 1024 + (t + 1) * 128],
                            self.warena[si][:, (c % 8) * 512:((c % 8) + 1) * 512],
                            start=(c == 0), stop=(c == 31))
                        if last_mm or hb:
                            v = r.inc("tensor", fn, "ct")
                            if last_mm:
                                mvs[t] = v
                            if hb:
                                ls = (o * 8 + q * 2 + h) % 8
                                self.half_free[ls] = v
                                if o + 1 < 8:
                                    loaded[(o + 1, q, h)] = self.load_half(
                                        self.Wop, o + 1, q, h)
                        else:
                            r.do("tensor", fn)
                for t in tiles:
                    self.emit_epilogue(t, o, t % 4, 1.0, mvs[t], resid_hx)
        return self.pop_tile_barriers()

    def emit_l1ffn(self, src, barrier_ac, resid_hx):
        r = self.r
        st = self.stat
        r.wait("gpsimd", "ct", r.cnt["ct"])
        r.inc("gpsimd", lambda e: e.dma_start(
            out=self.batt[:, 0:16384], in_=self.AwTp), "wlb", 16)
        r.inc("gpsimd", lambda e: e.dma_start(
            out=self.sE64, in_=self.E64d), "wlb", 16)
        r.inc("gpsimd", lambda e: e.dma_start(
            out=self.sWqTp, in_=self.WqTp), "wlb", 16)
        r.inc("gpsimd", lambda e: e.dma_start(
            out=self.skeysT, in_=self.keysT), "wlb", 16)
        wlb_all = r.cnt["wlb"]
        zgt_cv = [0] * NT
        for t, ap in self.ln_pipeline(src, barrier_ac):
            r.wait("tensor", "wlb", wlb_all)
            r.wait("tensor", "cv", self.xT_ready)
            r.wait("tensor", "cs", self.pf_free_cs[0])
            z_ct = None
            for c in range(32):
                fn = lambda e, c=c, t=t: e.matmul(
                    self.pfv(0),
                    self.xT[:, c * 1024 + t * 128:c * 1024 + (t + 1) * 128],
                    self.batt[:, c * 512:(c + 1) * 512],
                    start=(c == 0), stop=(c == 31))
                if c == 31:
                    z_ct = r.inc("tensor", fn, "ct")
                else:
                    r.do("tensor", fn)
            r.wait("scalar", "ct", z_ct)
            r.wait("scalar", "cv", getattr(self, "zg_cv", 0))
            zev = r.do("scalar", lambda e: e.activation(
                out=self.small1[:], in_=self.pfv(0), func=AF.Identity))
            self.pf_free_cs[0] = zev
            r.wait("tensor", "wlb", wlb_all)
            r.wait("tensor", "cs", self.pf_free_cs[1])
            r.wait("tensor", "cv", self.pf_free_cv[1])
            r.wait("tensor", "cv", self.xT_ready)
            qr_ct = None
            for c in range(32):
                fn = lambda e, c=c, t=t: e.matmul(
                    self.pfv(1)[:, 0:16],
                    self.xT[:, c * 1024 + t * 128:c * 1024 + (t + 1) * 128],
                    self.sWqTp[:, c * 16:(c + 1) * 16],
                    start=(c == 0), stop=(c == 31))
                if c == 31:
                    qr_ct = r.inc("tensor", fn, "ct")
                else:
                    r.do("tensor", fn)
            r.wait("scalar", "ct", qr_ct)
            qe = r.do("scalar", lambda e: e.activation(
                out=self.qrb[:], in_=self.pfv(1)[:, 0:16], func=AF.Identity))
            self.pf_free_cs[1] = qe
            r.wait("tensor", "cs", qe)
            r.wait("tensor", "cv", self.pb_free[1])
            qt_ct = r.inc("tensor", lambda e: e.transpose(
                self.pb[1][0:16, 0:128], self.qrb[:], self.sident[:]), "ct")
            r.wait("vector", "ct", qt_ct)
            qte = r.do("vector", lambda e: e.tensor_copy(
                out=self.qrT[:], in_=self.pb[1][0:16, 0:128]))
            self.pb_free[1] = qte
            r.wait("tensor", "cv", qte)
            r.wait("tensor", "cs", self.pf_free_cs[2])
            r.wait("tensor", "cv", self.pf_free_cv[2])
            sc_ct = r.inc("tensor", lambda e: e.matmul(
                self.pfv(2)[:, 0:64], self.qrT[:], self.skeysT[:],
                start=True, stop=True), "ct")
            r.wait("vector", "ct", sc_ct)
            sce = r.do("vector", lambda e: e.tensor_copy(
                out=self.l1s[:], in_=self.pfv(2)[:, 0:64]))
            self.pf_free_cv[2] = sce
            r.selfwait("vector")
            r.do("vector", lambda e: e.tensor_copy(out=self.l1c[:], in_=self.l1s[:]))
            r.selfwait("vector")
            r.do("vector", lambda e: e.max(out=self.sg8[:], in_=self.l1c[:]))
            r.selfwait("vector")
            r.do("vector", lambda e: e.match_replace(
                out=self.l1c[:], in_to_replace=self.sg8[:], in_values=self.l1c[:],
                imm_value=-1e30))
            r.selfwait("vector")
            r.do("vector", lambda e: e.max(out=self.sidxf[:], in_=self.l1c[:]))
            r.selfwait("vector")
            nmx = r.do("vector", lambda e: e.tensor_scalar(
                out=self.swm[:, 0:1], in0=self.sg8[:, 0:1], scalar1=-1.0,
                scalar2=None, op0=ALU.mult))
            r.wait("scalar", "cv", nmx)
            exv = r.do("scalar", lambda e: e.activation(
                out=self.l1c[:], in_=self.l1s[:], func=AF.Exp,
                bias=self.swm[:, 0:1]))
            r.wait("vector", "cs", exv)
            r.do("vector", lambda e: e.scalar_tensor_tensor(
                out=self.l1g[:], in0=self.l1s[:], scalar=self.sidxf[:, 7:8],
                in1=self.l1c[:], op0=ALU.is_ge, op1=ALU.mult,
                accum_out=self.swm[:, 1:2]))
            r.selfwait("vector")
            r.do("vector", lambda e: e.reciprocal(
                out=self.swm[:, 2:3], in_=self.swm[:, 1:2]))
            r.selfwait("vector")
            gbv = r.do("vector", lambda e: e.tensor_scalar(
                out=self.l1gb[:], in0=self.l1g[:], scalar1=self.swm[:, 2:3],
                scalar2=None, op0=ALU.mult))
            r.wait("tensor", "cv", gbv)
            r.wait("tensor", "cv", self.pb_free[0])
            gt_ct = r.inc("tensor", lambda e: e.transpose(
                self.pb[0][0:64, 0:128], self.l1gb[:], self.sident[:]), "ct")
            r.wait("vector", "ct", gt_ct)
            gte = r.do("vector", lambda e: e.tensor_copy(
                out=self.sgT[:], in_=self.pb[0][0:64, 0:128]))
            self.pb_free[0] = gte
            r.wait("tensor", "cv", gte)
            r.wait("tensor", "cs", self.pf_free_cs[3])
            r.wait("tensor", "cv", self.pf_free_cv[3])
            gf_ct = r.inc("tensor", lambda e: e.matmul(
                self.pfv(3), self.sgT[:], self.sE64[:],
                start=True, stop=True), "ct")
            r.wait("vector", "cs", zev)
            r.wait("vector", "ct", gf_ct)
            zgv = r.do("vector", lambda e: e.tensor_tensor(
                out=self.small2[:], in0=self.small1[:], in1=self.pfv(3),
                op=ALU.mult))
            self.pf_free_cv[3] = zgv
            self.zg_cv = zgv
            r.wait("tensor", "cv", zgv)
            r.wait("tensor", "cv", self.pb_free[1])
            for j in range(3):
                r.do("tensor", lambda e, j=j: e.transpose(
                    self.pb[1][:, j * 128:(j + 1) * 128],
                    self.small2[:, j * 128:(j + 1) * 128], self.sident[:]))
            zgt_ct = r.inc("tensor", lambda e: e.transpose(
                self.pb[1][:, 384:512], self.small2[:, 384:512],
                self.sident[:]), "ct")
            r.wait("vector", "ct", zgt_ct)
            zge = r.do("vector", lambda e, t=t: e.tensor_copy(
                out=self.bb[:, t * 512:(t + 1) * 512], in_=self.pb[1][:, 0:512]))
            self.pb_free[1] = zge
            zgt_cv[t] = zge
        for o in range(8):
            si = o % NWAR
            r.wait("gpsimd", "ct", self.war_free[si])
            bsrc = self.BwT[:, o * 512:(o + 1) * 512].rearrange(
                "(c p) n -> p c n", p=128)
            wv = r.inc("gpsimd", lambda e, si=si, bsrc=bsrc: e.dma_start(
                out=ap2(self.warena[si], 0, [[512, 4], [1, 512]]),
                in_=bsrc), f"wl{si}", 16)
            for t in range(NT):
                bank = t % 4
                r.wait("tensor", f"wl{si}", wv)
                r.wait("tensor", "cv", zgt_cv[t])
                r.wait("tensor", "cs", self.pf_free_cs[bank])
                r.wait("tensor", "cv", self.pf_free_cv[bank])
                for j in range(3):
                    r.do("tensor", lambda e, t=t, j=j, si=si, bank=bank: e.matmul(
                        self.pfv(bank),
                        self.bb[:, t * 512 + j * 128:t * 512 + (j + 1) * 128],
                        self.warena[si][:, j * 512:(j + 1) * 512],
                        start=(j == 0), stop=False))
                mv = r.inc("tensor", lambda e, t=t, si=si, bank=bank: e.matmul(
                    self.pfv(bank),
                    self.bb[:, t * 512 + 384:t * 512 + 512],
                    self.warena[si][:, 1536:2048],
                    start=False, stop=True), "ct")
                self.emit_epilogue(t, o, bank, 1.0, mv, resid_hx)
            self.war_free[si] = r.cnt["ct"]
        return self.pop_tile_barriers()


    def emit_l1attn(self, src, barrier_ac, resid_hx):
        r = self.r
        st = self.stat
        r.wait("gpsimd", "ct", r.cnt["ct"])
        for t, ap in self.ln_pipeline(src, barrier_ac):
            r.wait("sync", "cs", ap)
            r.inc("sync", lambda e, t=t: e.dma_start(
                out=self.vv[t * 128:(t + 1) * 128, 0:4096], in_=self.lnb[:]),
                "g1", 16)
            self.lnb_store_g1 = r.inc("sync", lambda e, t=t: e.dma_start(
                out=self.vv[t * 128:(t + 1) * 128, 4096:8192], in_=self.lnb[:]),
                "g1", 16)
        vv_all = r.cnt["g1"]
        self.lnb_store_g1 = 0
        if getattr(self, "l1attn_phases", 4) < 2:
            return tuple(r.cnt[f"ac{i}"] for i in range(4))
        pt_cv = 0
        for kc in range(NKC):
            sic = (2 * kc) % NWAR
            sis = (2 * kc + 1) % NWAR
            r.wait("gpsimd", "ct", self.war_free[sic])
            lc = (f"wl{sic}", r.inc("gpsimd", lambda e, kc=kc, sic=sic: e.dma_start(
                out=self.warena[sic][:], in_=self.COSp[kc]), f"wl{sic}", 16))
            r.wait("gpsimd", "ct", self.war_free[sis])
            ls = (f"wl{sis}", r.inc("gpsimd", lambda e, kc=kc, sis=sis: e.dma_start(
                out=self.warena[sis][:], in_=self.SINp[kc]), f"wl{sis}", 16))
            r.wait("tensor", *lc)
            r.wait("tensor", *ls)
            r.wait("tensor", "cv", self.xT_ready)
            for g in range(2):
                for re_im, sl in ((0, sic), (1, sis)):
                    bank = g * 2 + re_im
                    r.wait("tensor", "cs", self.pf_free_cs[bank])
                    mm_ct = None
                    for fc in range(32):
                        fn = lambda e, fc=fc, g=g, sl=sl, bank=bank: e.matmul(
                            self.pfv(bank),
                            self.warena[sl][:, fc * 128:(fc + 1) * 128],
                            self.xT[:, fc * 1024 + g * 512:fc * 1024 + g * 512 + 512],
                            start=(fc == 0), stop=(fc == 31))
                        if fc == 31:
                            mm_ct = r.inc("tensor", fn, "ct")
                        else:
                            r.do("tensor", fn)
                    self.last_fwd_ct = mm_ct
                r.wait("scalar", "ct", self.last_fwd_ct)
                r.wait("scalar", "cv", getattr(self, "evr_sq_cv", 0))
                s1 = r.do("scalar", lambda e, g=g: e.activation(
                    out=self.evr[:, 0:512], in_=self.pfv(g * 2), func=AF.Square))
                s2 = r.do("scalar", lambda e, g=g: e.activation(
                    out=self.evr[:, 512:1024], in_=self.pfv(g * 2 + 1),
                    func=AF.Square))
                self.pf_free_cs[g * 2] = s1
                self.pf_free_cs[g * 2 + 1] = s2
                r.wait("vector", "cs", s2)
                pt_cv = r.do("vector", lambda e, kc=kc, g=g: e.tensor_tensor(
                    out=self.batt[:, kc * 1024 + g * 512:kc * 1024 + g * 512 + 512],
                    in0=self.evr[:, 0:512], in1=self.evr[:, 512:1024],
                    op=ALU.add))
                self.evr_sq_cv = pt_cv
            self.war_free[sic] = r.cnt["ct"]
            self.war_free[sis] = r.cnt["ct"]
        if getattr(self, "l1attn_phases", 4) < 3:
            return tuple(r.cnt[f"ac{i}"] for i in range(4))
        corr_cv = [0] * NT
        for tc in range(32):
            si = tc % NWAR
            r.wait("gpsimd", "ct", self.war_free[si])
            li = (f"wl{si}", r.inc("gpsimd", lambda e, tc=tc, si=si: e.dma_start(
                out=self.warena[si][:, 0:KF], in_=self.ICOSTp[tc]), f"wl{si}", 16))
            r.wait("tensor", *li)
            if tc == 0:
                r.wait("tensor", "cv", pt_cv)
            for half in range(2):
                bank = (tc % 2) * 2 + half
                r.wait("tensor", "cs", self.pf_free_cs[bank])
                mm_ct = None
                for kc in range(NKC):
                    fn = lambda e, kc=kc, half=half, si=si, bank=bank: e.matmul(
                        self.pfv(bank),
                        self.warena[si][:, kc * 128:(kc + 1) * 128],
                        self.batt[:, kc * 1024 + half * 512:kc * 1024 + half * 512 + 512],
                        start=(kc == 0), stop=(kc == NKC - 1))
                    if kc == NKC - 1:
                        mm_ct = r.inc("tensor", fn, "ct")
                    else:
                        r.do("tensor", fn)
                r.wait("scalar", "ct", mm_ct)
                r.wait("scalar", "ct", getattr(self, "cbuf_free_ct", 0))
                ce = r.do("scalar", lambda e, half=half, bank=bank: e.activation(
                    out=self.cbuf[:, half * 512:(half + 1) * 512],
                    in_=self.pfv(bank), func=AF.Identity))
                self.pf_free_cs[bank] = ce
                bi = half
                r.wait("tensor", "cs", ce)
                r.wait("tensor", "cv", self.pb_free[bi])
                tr_ct = None
                for tt in range(4):
                    fn = lambda e, half=half, tt=tt, bi=bi: e.transpose(
                        self.pb[bi][:, tt * 128:(tt + 1) * 128],
                        self.cbuf[:, half * 512 + tt * 128:half * 512 + (tt + 1) * 128],
                        self.sident[:])
                    if tt == 3:
                        tr_ct = r.inc("tensor", fn, "ct")
                    else:
                        r.do("tensor", fn)
                if half == 1:
                    self.cbuf_free_ct = tr_ct
                r.wait("vector", "ct", tr_ct)

                def ev(e, half=half, tc=tc, bi=bi):
                    dst = ap2(self.xT, (4 * half) * 4096 + tc * 128,
                              [[4096, 4], [1, 128]])
                    return e.tensor_copy(out=dst, in_=self.pb[bi][:, 0:512])
                cve = r.do("vector", ev)
                self.pb_free[bi] = cve
                for tt in range(4):
                    corr_cv[half * 4 + tt] = cve
            self.war_free[si] = r.cnt["ct"]
        if getattr(self, "l1attn_phases", 4) < 4:
            return tuple(r.cnt[f"ac{i}"] for i in range(4))
        acc_free = 0
        for t in range(NT):
            corr = self.xT[:, t * 4096:(t + 1) * 4096]
            r.wait("vector", "cv", corr_cv[t])
            r.selfwait("vector")
            r.do("vector", lambda e, corr=corr: e.max(
                out=self.l1gb[:, 0:8], in_=corr))
            r.selfwait("vector")
            mi = r.do("vector", lambda e, corr=corr: e.max_index(
                out=self.sgidx[:], in_max=self.l1gb[:, 0:8], in_values=corr))
            r.selfwait("vector")
            nv = r.do("vector", lambda e: e.tensor_scalar(
                out=st[:, 173:174], in0=self.l1gb[:, 0:1], scalar1=-1.0,
                scalar2=None, op0=ALU.mult))
            r.wait("scalar", "cv", nv)
            exv = r.do("scalar", lambda e: e.activation(
                out=self.sg8[:], in_=self.l1gb[:, 0:8], func=AF.Exp,
                bias=st[:, 173:174], accum_out=st[:, 174:175]))
            r.wait("vector", "cs", exv)
            r.do("vector", lambda e: e.reciprocal(
                out=st[:, 175:176], in_=st[:, 174:175]))
            r.selfwait("vector")
            r.do("vector", lambda e: e.tensor_scalar(
                out=self.sidxf[:], in0=self.sg8[:], scalar1=st[:, 175:176],
                scalar2=None, op0=ALU.mult))
            r.do("vector", lambda e: e.tensor_copy(
                out=self.l1s[:, 0:8], in_=self.sgidx[:]))
            r.selfwait("vector")
            r.do("vector", lambda e: e.tensor_scalar(
                out=self.l1s[:, 8:16], in0=self.l1s[:, 0:8],
                scalar1=self.siotavv[:, 0:1], scalar2=None, op0=ALU.add))
            r.selfwait("vector")
            gi = r.do("vector", lambda e: e.tensor_copy(
                out=self.sgidx[:], in_=self.l1s[:, 8:16]))
            r.wait("gpsimd", "cv", gi)
            r.wait("gpsimd", "g1", vv_all)
            gsem_val = {}
            applied = {}

            def gath(j):
                gb = self.bb if j % 2 == 0 else self.lnb
                gs = "ga0" if j % 2 == 0 else "qt"
                if j >= 2:
                    r.wait("gpsimd", "cv", applied[j - 2])
                gsem_val[j] = r.inc("gpsimd",
                    lambda e, j=j, t=t, gb=gb: e.indirect_dma_start(
                        out=gb[:], out_offset=None, in_=self.vv,
                        in_offset=IndirectOffsetOnAxis(self.sgidx[:, j:j + 1], 1),
                        element_offset=t * 128 * 8192), gs, 16)

            def appl(j):
                gb = self.bb if j % 2 == 0 else self.lnb
                gs = "ga0" if j % 2 == 0 else "qt"
                r.wait("vector", gs, gsem_val[j])
                if j == 0:
                    r.wait("vector", "ac0", getattr(self, "acc_ac", 0))
                    av = r.do("vector", lambda e, gb=gb: e.tensor_scalar(
                        out=self.xgb[0][:], in0=gb[:], scalar1=self.sidxf[:, 0:1],
                        scalar2=None, op0=ALU.mult))
                else:
                    r.selfwait("vector")
                    av = r.do("vector", lambda e, gb=gb, j=j: e.scalar_tensor_tensor(
                        out=self.xgb[0][:], in0=gb[:], scalar=self.sidxf[:, j:j + 1],
                        in1=self.xgb[0][:], op0=ALU.mult, op1=ALU.add))
                applied[j] = av

            lvl = getattr(self, "l1attn_phases", 6)
            if lvl < 5:
                continue
            gath(0)
            gath(1)
            if lvl < 6:
                continue
            for j in range(8):
                appl(j)
                if j + 2 < 8:
                    gath(j + 2)
            av = applied[7]
            if getattr(self, "l1attn_phases", 7) < 7:
                continue
            r.wait("gpsimd", "cv", av)
            if resid_hx:
                r.wait("gpsimd", "hr", resid_hx)
            for ch in range(4):
                self.acc_ac = r.inc("gpsimd", lambda e, t=t, ch=ch: e.dma_start(
                    out=self.resid[t * 128:(t + 1) * 128, ch * 1024:(ch + 1) * 1024],
                    in_=self.xgb[0][:, ch * 1024:(ch + 1) * 1024],
                    accum_op=ALU.add), "ac0", 16)
            self.tile_acc.setdefault(t, []).append(("ac0", self.acc_ac))
        self.xf_extra[0] = [("ac0", r.cnt["ac0"])]
        self.lnb_gate = [("qt", r.cnt["qt"]), ("cv", r.cnt["cv"])]
        return self.pop_tile_barriers()

    def emit_final(self, barrier_ac):
        r = self.r
        if barrier_ac:
            for t, lst in sorted(barrier_ac.items()):
                for sem, val in lst:
                    r.wait("sync", sem, val)
        for t in range(NT):
            r.inc("sync", lambda e, t=t: e.dma_start(
                out=self.y[t * 128:(t + 1) * 128, :],
                in_=self.resid[t * 128:(t + 1) * 128, :]), "au", 16)
        r.wait("sync", "au", r.cnt["au"])
        for i in range(4):
            r.wait("gpsimd", f"ac{i}", r.cnt[f"ac{i}"])
        for i in range(8):
            r.wait("gpsimd", f"wl{i}", r.cnt[f"wl{i}"])
        r.wait("gpsimd", "wlb", r.cnt["wlb"])
        r.wait("vector", "cv", r.cnt["cv"])
        r.wait("scalar", "cs", r.cnt["cs"])
        r.wait("tensor", "ct", r.cnt["ct"])


def build_program(parts, core_id=0):
    m = MK(parts, core_id)
    m.emit_const_loads()
    m.emit_resid_init()
    bar = None
    first = True
    for p in parts:
        src = m.xin if first else m.resid
        if p.startswith("l0ffn"):
            bar = m.emit_l0ffn(src, bar, m.resid_init_hx)
        elif p.startswith("l0attn"):
            bar = m.emit_l0attn(src, bar, m.resid_init_hx)
        elif p.startswith("l1ffn"):
            bar = m.emit_l1ffn(src, bar, m.resid_init_hx)
        elif p.startswith("l1attn"):
            if "@" in p:
                m.l1attn_phases = int(p.split("@")[1])
            bar = m.emit_l1attn(src, bar, m.resid_init_hx)
        else:
            raise ValueError(p)
        first = False
    m.emit_final(bar)
    m.r.replay()
    m.close()
    return m



PARTS = ["l0ffn", "l0attn", "l0ffn2", "l1ffn", "l1attn", "l1ffn2"]


def kernel(**inputs):
    x0 = np.ascontiguousarray(
        np.asarray(inputs["L0_attn_x"], dtype=np.float32).reshape(NCHIP * TOK, D))
    consts = make_consts()
    prog = build_program(PARTS)
    declared = set()
    for alloc in prog.nc.m.functions[0].allocations:
        if isinstance(alloc, mybir.MemoryLocationSet) and alloc.kind == "ExternalInput":
            declared.add(alloc.memorylocations[0].name)
    in_maps = []
    for c in range(NCHIP):
        w = pack_weights(inputs, c)
        m = dict(xin=np.ascontiguousarray(x0[core_rows(c)]), **w, **consts)
        in_maps.append({k: v for k, v in m.items() if k in declared})
    res = run_bass_kernel_spmd(prog.nc, in_maps, core_ids=list(range(NCHIP)))
    out = np.empty((NCHIP * TOK, D), dtype=np.float32)
    for c in range(NCHIP):
        out[core_rows(c)] = res.results[c]["y"]
    return out.reshape(2, 4096, D)


# version 09
# speedup vs baseline: 1.0613x; 1.0613x over previous
import numpy as np
import ml_dtypes
import concourse.bass as bass
import concourse.mybir as mybir
from concourse.bass_utils import run_bass_kernel_spmd
from concourse.bass import IndirectOffsetOnAxis

F32 = mybir.dt.float32
BF16 = mybir.dt.bfloat16
U32 = mybir.dt.uint32
AF = mybir.ActivationFunctionType
ALU = mybir.AluOpType
AXX = mybir.AxisListType.X

D = 4096
NT = 8
TOK = NT * 128
NCHIP = 8
EPS = 1e-5
KF = 2176
NKC = KF // 128
MEAN_CNT = 3484.0


def ap2(base_ap, offset_elems, dims):
    return bass.AP(base_ap.tensor, base_ap.offset + offset_elems,
                   [[base_ap.ap[0][0], base_ap.ap[0][1]]] + dims)



def pack_panels(wt, n_o, o_width, n_q=4):
    kc_per_q = 32 // n_q
    w4 = wt.reshape(n_q, kc_per_q, 128, n_o, o_width)
    return np.ascontiguousarray(
        w4.transpose(3, 0, 2, 1, 4).reshape(n_o, n_q, 128, kc_per_q * o_width))


def pack_chunkrows(wt, width):
    return np.ascontiguousarray(
        wt.reshape(32, 128, width).transpose(1, 0, 2).reshape(128, 32 * width))


def core_rows(c):
    return np.arange(c * TOK, (c + 1) * TOK)


def make_consts():
    o = {}
    o["ident"] = np.eye(128, dtype=ml_dtypes.bfloat16)
    o["onescol"] = np.ones((1, 128), dtype=ml_dtypes.bfloat16)
    e8 = np.zeros((8, 128), np.float32)
    for e in range(8):
        e8[e, e * 16:(e + 1) * 16] = 2.0
    o["E8"] = e8.astype(ml_dtypes.bfloat16)
    e64 = np.zeros((64, 512), np.float32)
    for e in range(64):
        e64[e, e * 8:(e + 1) * 8] = 1.0
    o["E64"] = e64.astype(ml_dtypes.bfloat16)
    masks = np.zeros((4, 128, 512), np.float32)
    jj = np.arange(512)[None, :]
    pp = np.arange(128)[:, None]
    for di in range(4):
        masks[di] = np.where(jj <= di * 128 + pp, 0.0, -1e30)
    o["masks"] = masks.astype(ml_dtypes.bfloat16)
    o["iotavv"] = (np.arange(128, dtype=np.float32) * 8192.0).reshape(128, 1)
    k = np.arange(KF)
    f = np.arange(D)
    ang = 2.0 * np.pi * np.outer(f, k % D) / D
    cosm = np.cos(ang)
    sinm = np.sin(ang)
    cosm[:, 2049:] = 0.0
    sinm[:, 2049:] = 0.0
    o["COSp"] = np.ascontiguousarray(
        cosm.reshape(32, 128, NKC, 128).transpose(2, 1, 0, 3).reshape(NKC, 128, D)
    ).astype(ml_dtypes.bfloat16)
    o["SINp"] = np.ascontiguousarray(
        sinm.reshape(32, 128, NKC, 128).transpose(2, 1, 0, 3).reshape(NKC, 128, D)
    ).astype(ml_dtypes.bfloat16)
    wk = np.where((k == 0) | (k == 2048), 1.0, 2.0)
    wk[k > 2048] = 0.0
    tau = np.arange(D)
    angi = 2.0 * np.pi * np.outer(k % D, tau) / D
    icos = (wk[:, None] / D) * np.cos(angi)
    o["ICOSTp"] = np.ascontiguousarray(
        icos.reshape(NKC, 128, 32, 128).transpose(2, 1, 0, 3).reshape(32, 128, KF)
    ).astype(ml_dtypes.bfloat16)
    return o


def pack_weights(ins, c):
    w = {}
    WT = np.ascontiguousarray(ins["L0_ffn_W"].T)
    w["Wp"] = pack_panels(WT, 8, 512)
    w["Wqp"] = pack_panels(np.ascontiguousarray(ins["L0_attn_wq"]), 8, 512)
    w["Wop"] = pack_panels(np.ascontiguousarray(ins["L0_attn_wo"]), 8, 512)
    w["wkvp"] = pack_chunkrows(
        np.concatenate([ins["L0_attn_wk"], ins["L0_attn_wv"]], axis=1), 256)
    w["AallTp"] = pack_chunkrows(
        np.ascontiguousarray(ins["L0_ffn_A"].reshape(128, D).T), 128)
    w["BallT"] = np.ascontiguousarray(
        ins["L0_ffn_B"].transpose(0, 2, 1).reshape(128, D))
    w["brow"] = np.ascontiguousarray(ins["L0_ffn_b"].reshape(1, D))
    w["rWTp"] = pack_chunkrows(np.ascontiguousarray(ins["L0_ffn_router_W"].T), 8)
    w["rb2"] = np.ascontiguousarray(
        np.broadcast_to(ins["L0_ffn_router_b"].reshape(1, 8), (2, 8)))
    w["AwTp"] = pack_chunkrows(np.ascontiguousarray(ins["L1_ffn_A_w"].T), 512)
    w["BwT"] = np.ascontiguousarray(ins["L1_ffn_B_w"].T)
    w["WqTp"] = pack_chunkrows(np.ascontiguousarray(ins["L1_ffn_Wq_w"].T), 16)
    w["keysT"] = np.ascontiguousarray(ins["L1_ffn_keys"].T)
    rows = core_rows(c)
    tok = rows % 4096
    beta = c // 4
    m = ((tok >= 611) & (tok <= 4094)).astype(np.float32)
    rm = np.zeros((NT, 128, 2), np.float32)
    rm[:, :, beta] = m.reshape(NT, 128)
    w["rmask"] = np.ascontiguousarray(rm.transpose(1, 0, 2).reshape(128, NT * 2))
    bs = np.zeros((2, 1), np.float32)
    bs[beta, 0] = 1.0
    w["bselc"] = bs
    w["wqh"] = pack_chunkrows(
        np.ascontiguousarray(ins["L0_attn_wq"][:, c * 512:(c + 1) * 512]), 512)
    aoff = np.zeros((128, 64), np.uint32)
    for t in range(NT):
        for e in range(8):
            aoff[:, t * 8 + e] = (np.arange(128) + (c // 4) * 32768 + e * 4096
                                  + (c % 4) * 1024 + t * 128) * 512
    w["aoff"] = aoff
    return w



SEMS = (["qx0", "qx1", "hr", "hs", "qxr", "cq", "au", "qt", "g0", "g1", "cc",
         "wlb", "kT0", "kT1", "ct", "cs", "cv", "cg", "ga0"]
        + [f"wl{i}" for i in range(8)] + [f"ac{i}" for i in range(4)])


class Rec:
    def __init__(self, nc):
        self.nc = nc
        self.ops = []
        self.cnt = {s: 0 for s in SEMS}
        self.semh = {}

    AUTO = {"vector": "cv", "scalar": "cs", "gpsimd": "cg"}

    def do(self, eng, fn):
        sem = self.AUTO.get(eng)
        if sem is None:
            self.ops.append((eng, fn))
            return None
        return self.inc(eng, fn, sem)

    def selfwait(self, eng):
        sem = self.AUTO[eng]
        self.wait(eng, sem, self.cnt[sem])

    def inc(self, eng, fn, sem, k=1):
        h = self.semh[sem]

        def f(e, fn=fn, h=h, k=k):
            fn(e).then_inc(h, k)
        self.ops.append((eng, f))
        self.cnt[sem] += k
        return self.cnt[sem]

    def wait(self, eng, sem, val):
        if val is None or val <= 0:
            return
        h = self.semh[sem]
        self.ops.append((eng, lambda e, h=h, v=val: e.wait_ge(h, v)))

    def replay(self):
        with self.nc.Block() as block:
            secs = dict(sync=block.sync, tensor=block.tensor, vector=block.vector,
                        scalar=block.scalar, gpsimd=block.gpsimd)
            i, n = 0, len(self.ops)
            while i < n:
                eng = self.ops[i][0]
                j = i
                while j < n and self.ops[j][0] == eng:
                    j += 1
                chunk = self.ops[i:j]

                def body(e, chunk=chunk):
                    for _, fn in chunk:
                        fn(e)
                secs[eng](body)
                i = j



NWAR = 4


class MK:
    def __init__(self, parts, core_id=0):
        self.parts = parts
        self.core_id = core_id
        nc = bass.Bass("TRN2", target_bir_lowering=False)
        self.nc = nc
        self._es = []
        dt = nc.dram_tensor
        need = set()
        for p in parts:
            need.add(p[:6] if p.startswith("l0attn") or p.startswith("l1attn")
                     else p[:5])
        self.need = need
        self.xin = dt("xin", [TOK, D], F32, kind="ExternalInput").ap()
        self.y = dt("y", [TOK, D], F32, kind="ExternalOutput").ap()
        self.resid = dt("resid", [TOK, D], F32, kind="Internal").ap()
        if "l0ffn" in need:
            self.Wp = dt("Wp", [8, 4, 128, 4096], F32, kind="ExternalInput").ap()
            self.AallTp = dt("AallTp", [128, 4096], F32, kind="ExternalInput").ap()
            self.BallT = dt("BallT", [128, 4096], F32, kind="ExternalInput").ap()
            self.brow = dt("brow", [1, 4096], F32, kind="ExternalInput").ap()
            self.rWTp = dt("rWTp", [128, 256], F32, kind="ExternalInput").ap()
            self.rb2 = dt("rb2", [2, 8], F32, kind="ExternalInput").ap()
            self.rmask = dt("rmask", [128, NT * 2], F32, kind="ExternalInput").ap()
            self.bselc = dt("bselc", [2, 1], F32, kind="ExternalInput").ap()
            self.rsum_in = dt("rsum_in", [128, 64], F32, kind="Internal").ap()
            self.rsum_out = dt("rsum_out", [128, 64], F32, kind="Internal").ap()
        if "l0attn" in need:
            self.Wqp = dt("Wqp", [8, 4, 128, 4096], F32, kind="ExternalInput").ap()
            self.Wop = dt("Wop", [8, 4, 128, 4096], F32, kind="ExternalInput").ap()
            self.wkvp = dt("wkvp", [128, 8192], F32, kind="ExternalInput").ap()
            self.wqh = dt("wqh", [128, 16384], F32, kind="ExternalInput").ap()
            self.x2loc = dt("x2loc", [TOK, D], BF16, kind="Internal").ap()
            self.xg = dt("xg", [NCHIP * TOK, D], BF16, kind="Internal").ap()
            self.aod = dt("aod", [NCHIP * TOK, 512], BF16, kind="Internal").ap()
            self.aog = dt("aog", [NCHIP * NCHIP * TOK, 512], BF16, kind="Internal").ap()
            self.aoff = dt("aoff", [128, 64], U32, kind="ExternalInput").ap()
            self.masksc = dt("masks", [4, 128, 512], BF16, kind="ExternalInput").ap()
            self.kvloc = dt("kvloc", [TOK, 256], BF16, kind="Internal").ap()
            self.kvg = dt("kvg", [NCHIP * TOK, 256], BF16, kind="Internal").ap()

        if "l1ffn" in need:
            self.AwTp = dt("AwTp", [128, 16384], F32, kind="ExternalInput").ap()
            self.BwT = dt("BwT", [512, 4096], F32, kind="ExternalInput").ap()
            self.WqTp = dt("WqTp", [128, 512], F32, kind="ExternalInput").ap()
            self.keysT = dt("keysT", [16, 64], F32, kind="ExternalInput").ap()
            self.E64d = dt("E64", [64, 512], BF16, kind="ExternalInput").ap()
        if "l1attn" in need:
            self.iotavv = dt("iotavv", [128, 1], F32, kind="ExternalInput").ap()
            self.COSp = dt("COSp", [NKC, 128, 4096], BF16, kind="ExternalInput").ap()
            self.SINp = dt("SINp", [NKC, 128, 4096], BF16, kind="ExternalInput").ap()
            self.ICOSTp = dt("ICOSTp", [32, 128, KF], BF16, kind="ExternalInput").ap()
            self.vv = dt("vv", [TOK, 8192], BF16, kind="Internal").ap()
        self.identd = dt("ident", [128, 128], BF16, kind="ExternalInput").ap()
        self.onescold = dt("onescol", [1, 128], BF16, kind="ExternalInput").ap()
        self.E8d = dt("E8", [8, 128], BF16, kind="ExternalInput").ap()

        sb = nc.alloc_sbuf_tensor
        self.xf = [sb(f"xf{i}", [128, D], F32).ap() for i in range(2)]
        _xf0_addr = nc.lookup_mloc(self.xf[0].tensor).addr
        self.xgb = [nc.alloc_sbuf_tensor_at(
            f"xgb{i}", [128, D], BF16, offset=_xf0_addr + i * 8192).ap()
            for i in range(2)]
        self.lnb = sb("lnb", [128, D], BF16).ap()
        self.xT = sb("xT", [128, 32 * 1024], BF16).ap()
        self.evr = sb("evr", [128, 2048], F32).ap()
        self.bb = sb("bb", [128, D], BF16).ap()
        self.batt = sb("batt", [128, 20480], BF16).ap()
        self.warena = [sb(f"war{i}", [128, 4096], BF16).ap()
                       for i in range(NWAR)]
        self.stat = sb("stat", [128, 192], F32).ap()
        self.zu = sb("zu", [128, 2048], BF16).ap()
        self.zT = self.zu[:, 0:1024]
        self.zgT = self.zu[:, 1024:2048]
        self.smasks = self.zu[:, 0:2048]
        self.scr2k = sb("scr2k", [128, 1024], BF16).ap()
        self.small1 = self.scr2k[:, 0:512]
        self.small2 = self.scr2k[:, 512:1024]
        self.cbuf = self.scr2k[:, 0:1024]
        self.sident = sb("sident", [128, 128], BF16).ap()
        self.sones = sb("sones", [1, 128], BF16).ap()
        self.sE8 = sb("sE8", [8, 128], BF16).ap()
        self.sE64 = self.batt[0:64, 16384:16896]
        self.sWqTp = self.batt[:, 16896:17408]
        self.skeysT = self.batt[0:16, 17408:17472]
        self.srWTp = self.batt[:, 8192:8448]
        self.kvb = self.batt[:, 0:256]
        self.siotavv = sb("siotavv", [128, 1], F32).ap()
        self.srb2 = sb("srb2", [2, 8], F32).ap()
        self.sq2 = sb("sq2", [128, 64], F32).ap()
        self.sbsel = sb("sbsel", [2, 1], F32).ap()
        self.saoff = sb("saoff", [128, 64], U32).ap()
        self.sbselb = sb("sbselb", [2, 1], BF16).ap()
        self.sq2b = self.batt[:, 8448:8512]
        self.sg8 = sb("sg8", [128, 8], F32).ap()
        self.sidxf = sb("sidxf", [128, 8], F32).ap()
        self.swm = sb("swm", [128, 8], F32).ap()
        self.sgidx = sb("sgidx", [128, 8], U32).ap()
        self.l1s = sb("l1s", [128, 64], F32).ap()
        self.l1c = sb("l1c", [128, 64], F32).ap()
        self.l1g = sb("l1g", [128, 64], F32).ap()
        self.l1gb = sb("l1gb", [128, 64], BF16).ap()
        self.sgT = sb("sgT", [64, 128], BF16).ap()
        self.qrb = sb("qrb", [128, 16], BF16).ap()
        self.qrT = sb("qrT", [16, 128], BF16).ap()
        self.sbrow = self.bb[0:1, :]

        ps = nc.alloc_psum_tensor
        self.psc = ps("psc", [128, 2048], F32).ap()
        self.pf45 = [ps(f"pf{i}", [128, 512], F32).ap() for i in range(2)]
        self.pb = [ps(f"pbk{i}", [128, 1024], BF16).ap() for i in range(2)]

        self.r = Rec(nc)
        for s in SEMS:
            self.r.semh[s] = self._enter(nc.semaphore(s))

        self.pb_free = [0, 0]
        self.pf_free_cs = [0] * 6
        self.pf_free_cv = [0] * 6
        self.xf_free_cs = [0, 0]
        self.xf_free_cv = [0, 0]
        self.war_free = [0] * NWAR
        self.half_free = [0] * 8
        self.evr_free = [0, 0, 0, 0]
        self.lnb_free = 0
        self.xT_ready = 0
        self.evr_rr = 0
        self.ln_nm = [0, 0]
        self.ln_vv = [0, 0]
        self.tile_acc = {}
        self.xf_extra = {}
        self.lnb_gate = []

    def pop_tile_barriers(self):
        out = {}
        for t, lst in self.tile_acc.items():
            agg = {}
            for sem, val in lst:
                agg[sem] = max(agg.get(sem, 0), val)
            out[t] = sorted(agg.items())
        self.tile_acc = {}
        return out

    def pfv(self, i):
        if i < 4:
            return self.psc[:, i * 512:(i + 1) * 512]
        return self.pf45[i - 4][:]

    def _enter(self, cm):
        v = cm.__enter__()
        self._es.append(cm)
        return v

    def close(self):
        for cm in reversed(self._es):
            cm.__exit__(None, None, None)

    def emit_const_loads(self):
        r = self.r
        pairs = [(self.sident, self.identd), (self.sones, self.onescold),
                 (self.sE8, self.E8d)]
        if "l0ffn" in self.need:
            pairs += [(self.srb2, self.rb2)]
        if "l1attn" in self.need:
            pairs += [(self.siotavv, self.iotavv)]

        for dst, src in pairs:
            r.inc("sync", lambda e, d=dst, s=src: e.dma_start(out=d[:], in_=s),
                  "cq", 16)
        self.const_qx = r.cnt["cq"]

    def emit_resid_init(self):
        r = self.r
        for t in range(NT):
            r.inc("sync", lambda e, t=t: e.dma_start(
                out=self.resid[t * 128:(t + 1) * 128, :],
                in_=self.xin[t * 128:(t + 1) * 128, :]), "hr", 16)
        self.resid_init_hx = r.cnt["hr"]

    def emit_ln_a(self, t, src, barrier_ac):
        r = self.r
        tb = t % 2
        st = self.stat
        if barrier_ac:
            for sem, val in barrier_ac.get(t, []):
                r.wait("sync", sem, val)
        r.wait("sync", "cs", self.xf_free_cs[tb])
        r.wait("sync", "cv", self.xf_free_cv[tb])
        for sem, val in self.xf_extra.get(tb, []):
            r.wait("sync", sem, val)
        qsem = "qx0" if tb == 0 else "qx1"
        qv = r.inc("sync", lambda e, tb=tb, t=t, src=src: e.dma_start(
            out=self.xf[tb][:], in_=src[t * 128:(t + 1) * 128, :]), qsem, 16)
        bnb = tb * 48
        sc = 96 + tb * 24
        r.wait("vector", qsem, qv)
        for c in range(8):
            r.do("vector", lambda e, c=c, tb=tb, bnb=bnb: e.bn_stats(
                out=st[:, bnb + c * 6:bnb + (c + 1) * 6],
                in_=self.xf[tb][:, c * 512:(c + 1) * 512]))
        means = ap2(st, bnb + 1, [[3, 16]])
        vns = ap2(st, bnb + 2, [[3, 16]])
        r.selfwait("vector")
        r.do("vector", lambda e, means=means, sc=sc: e.reduce_sum(
            st[:, sc:sc + 1], means, axis=AXX))
        r.do("vector", lambda e, vns=vns, sc=sc: e.reduce_sum(
            st[:, sc + 1:sc + 2], vns, axis=AXX))
        r.do("vector", lambda e, means=means, sc=sc: e.tensor_tensor(
            out=st[:, sc + 8:sc + 24], in0=means, in1=means, op=ALU.mult))
        r.selfwait("vector")
        r.do("vector", lambda e, sc=sc: e.reduce_sum(
            st[:, sc + 2:sc + 3], st[:, sc + 8:sc + 24], axis=AXX))
        r.do("vector", lambda e, sc=sc: e.tensor_scalar(
            out=st[:, sc + 3:sc + 4], in0=st[:, sc:sc + 1], scalar1=1.0 / 16,
            scalar2=None, op0=ALU.mult))
        r.selfwait("vector")
        r.do("vector", lambda e, sc=sc: e.tensor_scalar(
            out=st[:, sc + 4:sc + 5], in0=st[:, sc + 3:sc + 4],
            scalar1=st[:, sc + 3:sc + 4], scalar2=-1.0,
            op0=ALU.mult, op1=ALU.mult))
        r.selfwait("vector")
        r.do("vector", lambda e, sc=sc: e.scalar_tensor_tensor(
            out=st[:, sc + 5:sc + 6], in0=st[:, sc + 1:sc + 2], scalar=1.0 / D,
            in1=st[:, sc + 4:sc + 5], op0=ALU.mult, op1=ALU.add))
        r.selfwait("vector")
        r.do("vector", lambda e, sc=sc: e.scalar_tensor_tensor(
            out=st[:, sc + 6:sc + 7], in0=st[:, sc + 2:sc + 3], scalar=1.0 / 16,
            in1=st[:, sc + 5:sc + 6], op0=ALU.mult, op1=ALU.add))
        r.selfwait("vector")
        vv_ = r.inc("vector", lambda e, sc=sc: e.tensor_scalar(
            out=st[:, sc + 6:sc + 7], in0=st[:, sc + 6:sc + 7], scalar1=EPS,
            scalar2=None, op0=ALU.add), "cv")
        self.ln_vv[tb] = vv_

    def emit_ln_a2(self, t):
        r = self.r
        tb = t % 2
        st = self.stat
        sc = 96 + tb * 24
        r.wait("scalar", "cv", self.ln_vv[tb])
        sv = r.inc("scalar", lambda e, sc=sc: e.activation(
            out=st[:, sc + 7:sc + 8], in_=st[:, sc + 6:sc + 7], func=AF.Sqrt), "cs")
        r.wait("vector", "cs", sv)
        r.do("vector", lambda e, sc=sc: e.reciprocal(
            out=st[:, sc + 20:sc + 21], in_=st[:, sc + 7:sc + 8]))
        r.selfwait("vector")
        nm = r.inc("vector", lambda e, sc=sc: e.tensor_scalar(
            out=st[:, sc + 21:sc + 22], in0=st[:, sc + 3:sc + 4],
            scalar1=st[:, sc + 20:sc + 21], scalar2=-1.0,
            op0=ALU.mult, op1=ALU.mult), "cv")
        self.xf_free_cv[tb] = nm
        self.ln_nm[tb] = nm

    def emit_ln_b(self, t):
        r = self.r
        tb = t % 2
        st = self.stat
        sc = 96 + tb * 24
        r.wait("scalar", "cv", self.ln_nm[tb])
        r.wait("scalar", "ct", self.lnb_free)
        for sem, val in self.lnb_gate:
            r.wait("scalar", sem, val)
        self.lnb_gate = []
        if getattr(self, "lnb_store_g1", 0):
            r.wait("scalar", "g1", self.lnb_store_g1)
        ap = r.inc("scalar", lambda e, tb=tb, sc=sc: e.activation(
            out=self.lnb[:], in_=self.xf[tb][:], func=AF.Identity,
            bias=st[:, sc + 21:sc + 22], scale=st[:, sc + 20:sc + 21]), "cs")
        self.xf_free_cs[tb] = ap
        r.wait("tensor", "cq", r.cnt["cq"])
        for g in range(4):
            bi = g % 2
            r.wait("tensor", "cs", ap)
            r.wait("tensor", "cv", self.pb_free[bi])
            for cc in range(7):
                c = g * 8 + cc
                r.do("tensor", lambda e, bi=bi, cc=cc, c=c: e.transpose(
                    self.pb[bi][:, cc * 128:(cc + 1) * 128],
                    self.lnb[:, c * 128:(c + 1) * 128], self.sident[:]))
            tv = r.inc("tensor", lambda e, bi=bi, g=g: e.transpose(
                self.pb[bi][:, 7 * 128:8 * 128],
                self.lnb[:, (g * 8 + 7) * 128:(g * 8 + 8) * 128],
                self.sident[:]), "ct")
            if g == 3:
                self.lnb_free = tv
            r.wait("vector", "ct", tv)

            def ev(e, bi=bi, g=g, t=t):
                dst = ap2(self.xT, g * 8 * 1024 + t * 128, [[1024, 8], [1, 128]])
                src_ = ap2(self.pb[bi], 0, [[128, 8], [1, 128]])
                return e.tensor_copy(out=dst, in_=src_)
            self.pb_free[bi] = r.inc("vector", ev, "cv")
        self.xT_ready = self.pb_free[1]
        return ap

    def ln_pipeline(self, src, barrier_ac):
        self.emit_ln_a(0, src, barrier_ac)
        self.emit_ln_a2(0)
        if NT > 1:
            self.emit_ln_a(1, src, barrier_ac)
        for t in range(NT):
            ap = self.emit_ln_b(t)
            if t + 1 < NT:
                self.emit_ln_a2(t + 1)
            if t + 2 < NT:
                self.emit_ln_a(t + 2, src, barrier_ac)
            yield t, ap

    def emit_epilogue(self, t, o, bank, scale, mm_ct, resid_hx=None):
        r = self.r
        ei = self.evr_rr % 4
        self.evr_rr += 1
        r.wait("vector", "ct", mm_ct)
        r.wait("vector", f"ac{ei}", self.evr_free[ei])
        ev = r.inc("vector", lambda e, ei=ei, bank=bank, scale=scale: e.tensor_scalar(
            out=self.evr[:, ei * 512:(ei + 1) * 512], in0=self.pfv(bank),
            scalar1=float(scale), scalar2=None, op0=ALU.mult), "cv")
        self.pf_free_cv[bank] = ev
        r.wait("gpsimd", "cv", ev)
        if resid_hx:
            r.wait("gpsimd", "hr", resid_hx)
        av = r.inc("gpsimd", lambda e, t=t, o=o, ei=ei: e.dma_start(
            out=self.resid[t * 128:(t + 1) * 128, o * 512:(o + 1) * 512],
            in_=self.evr[:, ei * 512:(ei + 1) * 512], accum_op=ALU.add),
            f"ac{ei}", 16)
        self.evr_free[ei] = av
        self.tile_acc.setdefault(t, []).append((f"ac{ei}", av))
        return av

    def load_panel(self, panel_src, o, q):
        r = self.r
        si = (o * 4 + q) % NWAR
        r.wait("gpsimd", "ct", max(self.war_free[si], self.half_free[2 * si],
                                   self.half_free[2 * si + 1]))
        return (f"wl{si}", r.inc(
            "gpsimd", lambda e, o=o, q=q, si=si, p=panel_src:
            e.dma_start(out=self.warena[si][:], in_=p[o, q]), f"wl{si}", 16))

    def load_half(self, panel_src, o, q, h):
        r = self.r
        ls = (o * 8 + q * 2 + h) % 8
        si, half = ls // 2, ls % 2
        r.wait("gpsimd", "ct", max(self.half_free[ls], self.war_free[si]))
        return (f"wl{ls}", r.inc(
            "gpsimd", lambda e, o=o, q=q, h=h, si=si, half=half, p=panel_src:
            e.dma_start(out=self.warena[si][:, half * 2048:(half + 1) * 2048],
                        in_=p[o, q][:, h * 2048:(h + 1) * 2048]), f"wl{ls}", 16))

    def emit_l0ffn(self, src, barrier_ac, resid_hx):
        r = self.r
        r.wait("gpsimd", "ct", r.cnt["ct"])
        aal = r.inc("gpsimd", lambda e: e.dma_start(
            out=self.batt[:, 0:4096], in_=self.AallTp), "wlb", 16)
        bal = r.inc("gpsimd", lambda e: e.dma_start(
            out=self.batt[:, 4096:8192], in_=self.BallT), "wlb", 16)
        rmv = r.inc("gpsimd", lambda e: e.dma_start(
            out=self.small2[:, 0:16], in_=self.rmask), "wlb", 16)
        bsv = r.inc("gpsimd", lambda e: e.dma_start(
            out=self.sbsel[:], in_=self.bselc), "wlb", 16)
        brw = r.inc("gpsimd", lambda e: e.dma_start(
            out=self.sbrow, in_=self.brow), "wlb", 16)
        rwv = r.inc("gpsimd", lambda e: e.dma_start(
            out=self.srWTp, in_=self.rWTp), "wlb", 16)
        wlb_all = r.cnt["wlb"]
        for t, ap in self.ln_pipeline(src, barrier_ac):
            r.wait("tensor", "wlb", wlb_all)
            if t == 0:
                r.wait("tensor", "cv", self.pf_free_cv[5])
            for c in range(32):
                fn = lambda e, c=c, t=t: e.matmul(
                    self.pfv(5)[:, c * 2:c * 2 + 2],
                    self.lnb[:, c * 128:(c + 1) * 128],
                    self.small2[:, t * 2:t * 2 + 2],
                    start=(t == 0 and c == 0), stop=(t == 7 and c == 31))
                if t == 7 and c == 31:
                    self.router_ct = r.inc("tensor", fn, "ct")
                else:
                    r.do("tensor", fn)
            r.wait("tensor", "cv", self.xT_ready)
            r.wait("tensor", "wlb", wlb_all)
            r.wait("tensor", "cs", self.pf_free_cs[0])
            r.wait("tensor", "cv", self.pf_free_cv[0])
            for c in range(31):
                r.do("tensor", lambda e, c=c, t=t: e.matmul(
                    self.pfv(0)[:, 0:128],
                    self.xT[:, c * 1024 + t * 128:c * 1024 + (t + 1) * 128],
                    self.batt[:, c * 128:(c + 1) * 128],
                    start=(c == 0), stop=False))
            zt = r.inc("tensor", lambda e, t=t: e.matmul(
                self.pfv(0)[:, 0:128],
                self.xT[:, 31 * 1024 + t * 128:31 * 1024 + (t + 1) * 128],
                self.batt[:, 31 * 128:32 * 128],
                start=False, stop=True), "ct")
            self.lnb_free = zt
            r.wait("vector", "ct", zt)
            ze = r.inc("vector", lambda e: e.tensor_copy(
                out=self.small1[:, 0:128], in_=self.pfv(0)[:, 0:128]), "cv")
            self.pf_free_cv[0] = ze
            r.wait("tensor", "cv", ze)
            r.wait("tensor", "cv", self.pb_free[0])
            ztr = r.inc("tensor", lambda e: e.transpose(
                self.pb[0][:, 0:128], self.small1[:, 0:128], self.sident[:]),
                "ct")
            r.wait("vector", "ct", ztr)
            self.pb_free[0] = r.inc("vector", lambda e, t=t: e.tensor_copy(
                out=self.zT[:, t * 128:(t + 1) * 128], in_=self.pb[0][:, 0:128]),
                "cv")
        st = self.stat
        r.wait("vector", "ct", self.router_ct)
        rs = r.inc("vector", lambda e: e.tensor_scalar(
            out=self.sq2[:], in0=self.pfv(5)[:, 0:64], scalar1=1.0 / MEAN_CNT,
            scalar2=None, op0=ALU.mult), "cv")
        self.pf_free_cv[5] = rs
        r.wait("sync", "cv", rs)
        s1 = r.inc("sync", lambda e: e.dma_start(out=self.rsum_in, in_=self.sq2[:]),
                   "hs", 16)
        r.wait("gpsimd", "hs", s1)
        ccv = r.inc("gpsimd", lambda e: e.collective_compute(
            "AllReduce", ALU.add, replica_groups=[list(range(NCHIP))],
            ins=[self.rsum_in.opt()], outs=[self.rsum_out.opt()]), "cc", 1)
        r.wait("sync", "cc", ccv)
        s2 = r.inc("sync", lambda e: e.dma_start(out=self.sq2[:], in_=self.rsum_out),
                   "qxr", 16)
        r.wait("vector", "qxr", s2)
        qb = r.inc("vector", lambda e: e.tensor_copy(
            out=self.sq2b[:], in_=self.sq2[:]), "cv")
        r.wait("tensor", "cv", qb)
        r.wait("tensor", "cq", self.const_qx)
        r.wait("tensor", "wlb", wlb_all)
        r.wait("tensor", "cv", self.pf_free_cv[1])
        lg = None
        for c in range(32):
            fn = lambda e, c=c: e.matmul(
                self.pfv(1)[0:2, 0:8],
                self.sq2b[:, c * 2:(c + 1) * 2],
                self.srWTp[:, c * 8:(c + 1) * 8],
                start=(c == 0), stop=(c == 31))
            if c == 31:
                lg = r.inc("tensor", fn, "ct")
            else:
                r.do("tensor", fn)
        r.wait("vector", "ct", lg)
        r.do("vector", lambda e: e.tensor_tensor(
            out=st[0:2, 144:152], in0=self.pfv(1)[0:2, 0:8], in1=self.srb2[:],
            op=ALU.add))
        r.selfwait("vector")
        r.do("vector", lambda e: e.reduce_max(
            st[0:2, 152:153], st[0:2, 144:152], axis=AXX))
        r.selfwait("vector")
        ng = r.inc("vector", lambda e: e.tensor_scalar(
            out=st[0:2, 153:154], in0=st[0:2, 152:153], scalar1=-1.0,
            scalar2=None, op0=ALU.mult), "cv")
        self.pf_free_cv[1] = ng
        r.wait("scalar", "cv", ng)
        ex = r.inc("scalar", lambda e: e.activation(
            out=st[0:2, 160:168], in_=st[0:2, 144:152],
            func=AF.Exp, bias=st[0:2, 153:154], accum_out=st[0:2, 168:169]), "cs")
        r.wait("vector", "cs", ex)
        r.do("vector", lambda e: e.reciprocal(
            out=st[0:2, 169:170], in_=st[0:2, 168:169]))
        r.selfwait("vector")
        r.do("vector", lambda e: e.tensor_scalar(
            out=st[0:2, 144:152], in0=st[0:2, 160:168], scalar1=st[0:2, 169:170],
            scalar2=None, op0=ALU.mult))
        r.selfwait("vector")
        rb = r.inc("vector", lambda e: e.tensor_copy(
            out=self.l1gb[0:2, 0:8], in_=st[0:2, 144:152]), "cv")
        r.selfwait("vector")
        rb2 = r.inc("vector", lambda e: e.tensor_copy(
            out=self.sbselb[:], in_=self.sbsel[:]), "cv")
        r.wait("tensor", "cv", rb)
        r.wait("tensor", "cv", rb2)
        r.wait("tensor", "cv", self.pf_free_cv[1])
        rs_ct = r.inc("tensor", lambda e: e.matmul(
            self.pfv(1)[0:8, 0:1], self.l1gb[0:2, 0:8], self.sbselb[:],
            start=True, stop=True), "ct")
        r.wait("vector", "ct", rs_ct)
        rse = r.inc("vector", lambda e: e.tensor_copy(
            out=self.sgT[0:8, 0:1], in_=self.pfv(1)[0:8, 0:1]), "cv")
        r.wait("tensor", "cv", rse)
        gv = r.inc("tensor", lambda e: e.matmul(
            self.pfv(1)[:, 0:1], self.sE8[:], self.sgT[0:8, 0:1],
            start=True, stop=True), "ct")
        r.wait("vector", "ct", gv)
        gve = r.inc("vector", lambda e: e.tensor_copy(
            out=st[:, 170:171], in_=self.pfv(1)[:, 0:1]), "cv")
        self.pf_free_cv[1] = gve
        r.selfwait("vector")
        gz = 0
        for t in range(NT):
            gz = r.inc("vector", lambda e, t=t: e.tensor_scalar(
                out=self.zgT[:, t * 128:(t + 1) * 128],
                in0=self.zT[:, t * 128:(t + 1) * 128],
                scalar1=st[:, 170:171], scalar2=None,
                op0=ALU.mult), "cv")
        loaded = {}
        for q in range(4):
            for h in range(2):
                loaded[(0, q, h)] = self.load_half(self.Wp, 0, q, h)
        r.wait("tensor", "cv", self.xT_ready)
        r.wait("tensor", "wlb", wlb_all)
        lora_waited = False
        for o in range(8):
            for grp in range(2):
                tiles = list(range(grp * 4, grp * 4 + 4))
                for c in range(32):
                    q = c // 8
                    h = (c % 8) // 4
                    si = (o * 4 + q) % NWAR
                    if c % 4 == 0:
                        r.wait("tensor", *loaded[(o, q, h)])
                    for ti, t in enumerate(tiles):
                        bank = t % 4
                        if c == 0:
                            r.wait("tensor", "cs", self.pf_free_cs[bank])
                            r.wait("tensor", "cv", self.pf_free_cv[bank])
                        fn = lambda e, c=c, t=t, si=si, bank=bank: e.matmul(
                            self.pfv(bank),
                            self.xT[:, c * 1024 + t * 128:c * 1024 + (t + 1) * 128],
                            self.warena[si][:, (c % 8) * 512:((c % 8) + 1) * 512],
                            start=(c == 0), stop=False)
                        if grp == 1 and c % 4 == 3 and ti == 3:
                            ls = (o * 8 + q * 2 + h) % 8
                            self.half_free[ls] = r.inc("tensor", fn, "ct")
                            if o + 1 < 8:
                                loaded[(o + 1, q, h)] = self.load_half(self.Wp, o + 1, q, h)
                        else:
                            r.do("tensor", fn)
                if not lora_waited:
                    r.wait("tensor", "cv", gz)
                    lora_waited = True
                for t in tiles:
                    bank = t % 4
                    r.do("tensor", lambda e, t=t, o=o, bank=bank: e.matmul(
                        self.pfv(bank), self.sones[0:1, :],
                        self.sbrow[0:1, o * 512:(o + 1) * 512],
                        start=False, stop=False))
                    mv = r.inc("tensor", lambda e, t=t, o=o, bank=bank: e.matmul(
                        self.pfv(bank), self.zgT[:, t * 128:(t + 1) * 128],
                        self.batt[:, 4096 + o * 512:4096 + (o + 1) * 512],
                        start=False, stop=True), "ct")
                    self.emit_epilogue(t, o, bank, 0.5, mv, resid_hx)
        return self.pop_tile_barriers()



    def emit_l0attn(self, src, barrier_ac, resid_hx):
        r = self.r
        st = self.stat
        SC = 0.0883883476483184
        r.wait("gpsimd", "ct", r.cnt["ct"])
        r.inc("gpsimd", lambda e: e.dma_start(
            out=self.batt[:, 0:8192], in_=self.wkvp), "wlb", 16)
        for i in range(4):
            r.wait("gpsimd", "ct", self.war_free[i])
            r.inc("gpsimd", lambda e, i=i: e.dma_start(
                out=self.warena[i][:],
                in_=self.wqh[:, i * 4096:(i + 1) * 4096]), "wlb", 16)
        r.inc("gpsimd", lambda e: e.dma_start(
            out=self.saoff[:], in_=self.aoff), "wlb", 16)
        wlb_all = r.cnt["wlb"]
        r.wait("sync", "ct", r.cnt["ct"])
        smv = r.inc("sync", lambda e: e.dma_start(
            out=ap2(self.smasks, 0, [[512, 4], [1, 512]]),
            in_=self.masksc.rearrange("m p j -> p m j")), "cq", 16)
        au0 = r.cnt["au"]
        kv_ct = 0
        for t, ap in self.ln_pipeline(src, barrier_ac):
            r.wait("sync", "cs", ap)
            sx = r.inc("sync", lambda e, t=t: e.dma_start(
                out=self.x2loc[t * 128:(t + 1) * 128, :], in_=self.lnb[:]),
                "g1", 16)
            self.lnb_store_g1 = sx
            if t % 4 == 3:
                h = t // 4
                r.wait("gpsimd", "g1", sx)
                r.inc("gpsimd", lambda e, h=h: e.collective_compute(
                    "AllGather", ALU.bypass, replica_groups=[list(range(NCHIP))],
                    ins=[self.x2loc[h * 512:(h + 1) * 512, :].opt()],
                    outs=[self.xg[h * 4096:(h + 1) * 4096, :].opt()]), "cc", 1)
            r.wait("tensor", "wlb", wlb_all)
            r.wait("tensor", "cv", self.xT_ready)
            if t == 0:
                r.wait("tensor", "cs", self.pf_free_cs[4])
                r.wait("tensor", "cv", self.pf_free_cv[4])
            for c in range(31):
                r.do("tensor", lambda e, c=c, t=t: e.matmul(
                    self.pfv(4)[:, 0:256],
                    self.xT[:, c * 1024 + t * 128:c * 1024 + (t + 1) * 128],
                    self.batt[:, c * 256:(c + 1) * 256],
                    start=(c == 0), stop=False))
            kv_ct = r.inc("tensor", lambda e, t=t: e.matmul(
                self.pfv(4)[:, 0:256],
                self.xT[:, 31 * 1024 + t * 128:31 * 1024 + (t + 1) * 128],
                self.batt[:, 31 * 256:32 * 256],
                start=False, stop=True), "ct")
            r.wait("vector", "ct", kv_ct)
            r.wait("vector", "au", getattr(self, "kv_store_au", 0))
            kve = r.do("vector", lambda e: e.tensor_copy(
                out=self.batt[:, 16640:16896], in_=self.pfv(4)[:, 0:256]))
            self.pf_free_cv[4] = kve
            r.wait("sync", "cv", kve)
            self.kv_store_au = r.inc("sync", lambda e, t=t: e.dma_start(
                out=self.kvloc[t * 128:(t + 1) * 128, :],
                in_=self.batt[:, 16640:16896]), "au", 16)
        r.wait("gpsimd", "au", r.cnt["au"])
        cc_all = r.inc("gpsimd", lambda e: e.collective_compute(
            "AllGather", ALU.bypass, replica_groups=[list(range(NCHIP))],
            ins=[self.kvloc.opt()], outs=[self.kvg.opt()]), "cc", 1)
        r.wait("vector", "ct", kv_ct)
        vmf = r.do("vector", lambda e: e.memset(self.batt[:, 8192:16640], 1.0))
        r.wait("sync", "cc", cc_all)
        r.wait("sync", "cv", vmf)
        for beta in range(2):
            ksem = f"kT{beta}"
            for B in range(16):
                r0 = beta * 4096 + B * 256
                r.inc("sync", lambda e, beta=beta, B=B, r0=r0: e.dma_start(
                    out=self.batt[:, beta * 4096 + B * 256:beta * 4096 + (B + 1) * 256],
                    in_=self.kvg[r0:r0 + 256, 0:128], transpose=True), ksem, 16)
                vdst = ap2(self.batt, 8192 + beta * 4224 + B * 2 * 132,
                           [[132, 2], [1, 128]])
                vsrc = self.kvg[r0:r0 + 256, 128:256].rearrange(
                    "(k p) h -> p k h", p=128)
                r.inc("sync", lambda e, vdst=vdst, vsrc=vsrc: e.dma_start(
                    out=vdst, in_=vsrc), ksem, 16)
        kt_all = [r.cnt["kT0"], r.cnt["kT1"]]
        xgb_free = [0, 0]
        xgt_cv = [0, 0]
        aod0 = r.cnt["au"]
        for qi in range(64):
            beta, i = qi // 32, qi % 32
            nkb = i // 4 + 1
            nch = i + 1
            di = i % 4
            p2 = qi % 2
            qsem = "qx0" if p2 == 0 else "qx1"
            r.wait("sync", "ct", xgb_free[p2])
            g_l = qi % 8
            xgrow = (g_l // 4) * 4096 + (qi // 8) * 512 + (g_l % 4) * 128
            qv = r.inc("sync", lambda e, xgrow=xgrow, p2=p2: e.dma_start(
                out=self.xgb[p2][:], in_=self.xg[xgrow:xgrow + 128, :]),
                qsem, 16)
            r.wait("tensor", qsem, qv)
            for g in range(4):
                bi = g % 2
                r.wait("tensor", "cv", self.pb_free[bi])
                for cc in range(7):
                    c = g * 8 + cc
                    r.do("tensor", lambda e, bi=bi, cc=cc, c=c, p2=p2: e.transpose(
                        self.pb[bi][:, cc * 128:(cc + 1) * 128],
                        self.xgb[p2][:, c * 128:(c + 1) * 128], self.sident[:]))
                tv = r.inc("tensor", lambda e, bi=bi, g=g, p2=p2: e.transpose(
                    self.pb[bi][:, 7 * 128:8 * 128],
                    self.xgb[p2][:, (g * 8 + 7) * 128:(g * 8 + 8) * 128],
                    self.sident[:]), "ct")
                if g == 3:
                    xgb_free[p2] = tv
                r.wait("vector", "ct", tv)
                self.pb_free[bi] = r.do("vector", lambda e, bi=bi, g=g, p2=p2:
                    e.tensor_copy(
                        out=self.xT[:, p2 * 4096 + g * 1024:p2 * 4096 + (g + 1) * 1024],
                        in_=self.pb[bi][:]))
            xgt_cv[p2] = self.pb_free[1]
            r.wait("tensor", "cv", xgt_cv[p2])
            r.wait("tensor", "cs", self.pf_free_cs[5])
            r.wait("tensor", "cv", self.pf_free_cv[5])
            qp_ct = None
            for c in range(32):
                fn = lambda e, c=c, p2=p2: e.matmul(
                    self.pfv(5),
                    self.xT[:, p2 * 4096 + c * 128:p2 * 4096 + (c + 1) * 128],
                    self.warena[c // 8][:, (c % 8) * 512:((c % 8) + 1) * 512],
                    start=(c == 0), stop=(c == 31))
                if c == 31:
                    qp_ct = r.inc("tensor", fn, "ct")
                else:
                    r.do("tensor", fn)
            r.wait("scalar", "ct", qp_ct)
            qe = r.do("scalar", lambda e: e.activation(
                out=self.scr2k[:, 0:512], in_=self.pfv(5), func=AF.Identity))
            self.pf_free_cs[5] = qe
            r.wait("tensor", "cs", qe)
            r.wait("tensor", "cv", self.pb_free[0])
            qt_ct = None
            for hl in range(4):
                fn = lambda e, hl=hl: e.transpose(
                    self.pb[0][:, hl * 128:(hl + 1) * 128],
                    self.scr2k[:, hl * 128:(hl + 1) * 128], self.sident[:])
                if hl == 3:
                    qt_ct = r.inc("tensor", fn, "ct")
                else:
                    r.do("tensor", fn)
            r.wait("vector", "ct", qt_ct)
            qte = r.do("vector", lambda e: e.tensor_copy(
                out=self.scr2k[:, 512:1024], in_=self.pb[0][:, 0:512]))
            self.pb_free[0] = qte
            r.wait("tensor", "cv", qte)
            r.wait("tensor", "cq", r.cnt["cq"])
            r.wait("tensor", f"kT{beta}", kt_all[beta])
            jobs = []
            for hl in range(4):
                for pair0 in range(0, nkb, 2):
                    jobs.append((hl, pair0))

            def emit_scores(idx):
                hl, pair0 = jobs[idx]
                half = idx % 2
                kbs = [kb for kb in (pair0, pair0 + 1) if kb < nkb]
                w = 512 * len(kbs)
                sl = idx % 3
                sc_ct = None
                for ki, kb in enumerate(kbs):
                    bank = half * 2 + ki
                    r.wait("tensor", "cs", self.pf_free_cs[bank])
                    r.wait("tensor", "cv", self.pf_free_cv[bank])
                    diag = (kb == nkb - 1)
                    fn = lambda e, kb=kb, hl=hl, beta=beta, bank=bank, diag=diag: \
                        e.matmul(
                            self.pfv(bank),
                            self.scr2k[:, 512 + hl * 128:512 + (hl + 1) * 128],
                            self.batt[:, beta * 4096 + kb * 512:beta * 4096 + (kb + 1) * 512],
                            start=True, stop=not diag)
                    if diag:
                        r.do("tensor", fn)
                        fnm = lambda e, bank=bank, di=di: e.matmul(
                            self.pfv(bank), self.sident[:],
                            self.smasks[:, di * 512:(di + 1) * 512],
                            start=False, stop=True)
                        sc_ct = r.inc("tensor", fnm, "ct")
                    elif ki == len(kbs) - 1:
                        sc_ct = r.inc("tensor", fn, "ct")
                    else:
                        r.do("tensor", fn)
                r.wait("scalar", "ct", sc_ct)
                if idx >= 3:
                    r.wait("scalar", "ct", self.ptr_ct[idx - 3])
                elif getattr(self, "prev_qi_ptr", 0):
                    r.wait("scalar", "ct", self.prev_qi_ptr)
                epx = r.do("scalar", lambda e, half=half, sl=sl, w=w: e.activation(
                    out=self.bb[:, sl * 1024:sl * 1024 + w],
                    in_=self.psc[:, half * 1024:half * 1024 + w],
                    func=AF.Exp, scale=SC))
                for ki in range(len(kbs)):
                    self.pf_free_cs[half * 2 + ki] = epx
                self.exp_cs[idx] = epx

            def emit_T(idx):
                hl, pair0 = jobs[idx]
                kbs = [kb for kb in (pair0, pair0 + 1) if kb < nkb]
                sl = idx % 3
                ncg = min(len(kbs) * 4, nch - pair0 * 4)
                pslot = idx % 2
                bi = idx % 2
                r.wait("tensor", "cs", self.exp_cs[idx])
                r.wait("tensor", "cv", self.pb_free[bi])
                pt_ct = None
                for j in range(ncg):
                    fn = lambda e, bi=bi, j=j, sl=sl: e.transpose(
                        self.pb[bi][:, j * 128:(j + 1) * 128],
                        self.bb[:, sl * 1024 + j * 128:sl * 1024 + (j + 1) * 128],
                        self.sident[:])
                    if j == ncg - 1:
                        pt_ct = r.inc("tensor", fn, "ct")
                    else:
                        r.do("tensor", fn)
                self.ptr_ct[idx] = pt_ct
                r.wait("vector", "ct", pt_ct)
                pte = r.do("vector", lambda e, bi=bi, ncg=ncg, pslot=pslot: e.tensor_copy(
                    out=self.xT[:, 8192 + pslot * 1024:8192 + pslot * 1024 + ncg * 128],
                    in_=self.pb[bi][:, 0:ncg * 128]))
                self.pb_free[bi] = pte
                self.pte_cv[idx] = pte

            def emit_pv(idx):
                hl, pair0 = jobs[idx]
                kbs = [kb for kb in (pair0, pair0 + 1) if kb < nkb]
                ncg = min(len(kbs) * 4, nch - pair0 * 4)
                pslot = idx % 2
                pvbank = 4 + hl % 2
                r.wait("tensor", "cv", self.pte_cv[idx])
                if pair0 == 0:
                    r.wait("tensor", "cs", self.pf_free_cs[pvbank])
                    r.wait("tensor", "cv", self.pf_free_cv[pvbank])
                pv_ct = None
                for j in range(ncg):
                    c = pair0 * 4 + j
                    fn = lambda e, j=j, c=c, beta=beta, nch=nch, pslot=pslot, pvbank=pvbank: \
                        e.matmul(
                            self.pfv(pvbank)[:, 0:132],
                            self.xT[:, 8192 + pslot * 1024 + j * 128:8192 + pslot * 1024 + (j + 1) * 128],
                            self.batt[:, 8192 + beta * 4224 + c * 132:8192 + beta * 4224 + (c + 1) * 132],
                            start=(c == 0), stop=(c == nch - 1))
                    if c == nch - 1:
                        pv_ct = r.inc("tensor", fn, "ct")
                    else:
                        r.do("tensor", fn)
                if pv_ct is not None:
                    r.wait("vector", "ct", pv_ct)
                    dn = r.do("vector", lambda e, hl=hl, pvbank=pvbank: e.reciprocal(
                        out=st[:, 176 + hl:177 + hl], in_=self.pfv(pvbank)[:, 128:129]))
                    r.wait("scalar", "cv", dn)
                    if hl == 0:
                        r.wait("scalar", "g0", getattr(self, "aod_g0", 0))
                    ave = r.do("scalar", lambda e, hl=hl, pvbank=pvbank: e.activation(
                        out=self.lnb[:, hl * 128:(hl + 1) * 128],
                        in_=self.pfv(pvbank)[:, 0:128], func=AF.Identity,
                        scale=st[:, 176 + hl:177 + hl]))
                    self.pf_free_cs[pvbank] = ave
                    self.pf_free_cv[pvbank] = dn
                    self.aob_cs[hl] = ave

            self.ptr_ct = {}
            self.exp_cs = {}
            self.aob_cs = {}
            self.pte_cv = {}
            nj = len(jobs)
            for idx in range(nj):
                emit_scores(idx)
                if idx >= 1:
                    emit_T(idx - 1)
                if idx >= 2:
                    emit_pv(idx - 2)
            emit_T(nj - 1)
            if nj >= 2:
                emit_pv(nj - 2)
            emit_pv(nj - 1)
            self.prev_qi_ptr = self.ptr_ct[nj - 1]
            r.wait("gpsimd", "cs", max(self.aob_cs.values()))
            self.aod_g0 = r.inc("gpsimd", lambda e, qi=qi: e.dma_start(
                out=self.aod[qi * 128:(qi + 1) * 128, :], in_=self.lnb[:, 0:512]),
                "g0", 16)
            if qi % 32 == 31:
                g = qi // 32
                r.wait("gpsimd", "g0", self.aod_g0)
                r.inc("gpsimd", lambda e, g=g: e.collective_compute(
                    "AllGather", ALU.bypass, replica_groups=[list(range(NCHIP))],
                    ins=[self.aod[g * 4096:(g + 1) * 4096, :].opt()],
                    outs=[self.aog[g * 32768:(g + 1) * 32768, :].opt()]), "cc", 1)
        aoe_free = 0
        loaded = {}
        for i in range(4):
            self.war_free[i] = r.cnt["ct"]
        for i in range(8):
            self.half_free[i] = r.cnt["ct"]
        for q in range(4):
            for h in range(2):
                loaded[(0, q, h)] = self.load_half(self.Wop, 0, q, h)
        aoT_cv = [0] * NT
        r.wait("gpsimd", "cc", r.cnt["cc"])
        for t in range(NT):
            r.wait("gpsimd", "ct", aoe_free)
            for eidx in range(8):
                av = r.inc("gpsimd", lambda e, t=t, eidx=eidx: e.indirect_dma_start(
                    out=self.lnb[:, eidx * 512:(eidx + 1) * 512], out_offset=None,
                    in_=self.aog,
                    in_offset=IndirectOffsetOnAxis(
                        self.saoff[:, t * 8 + eidx:t * 8 + eidx + 1], 1)),
                    "qt", 16)
            r.wait("tensor", "qt", av)
            for g in range(4):
                bi = g % 2
                r.wait("tensor", "cv", self.pb_free[bi])
                for cc in range(7):
                    c = g * 8 + cc
                    r.do("tensor", lambda e, bi=bi, cc=cc, c=c: e.transpose(
                        self.pb[bi][:, cc * 128:(cc + 1) * 128],
                        self.lnb[:, c * 128:(c + 1) * 128], self.sident[:]))
                tv = r.inc("tensor", lambda e, bi=bi, g=g: e.transpose(
                    self.pb[bi][:, 7 * 128:8 * 128],
                    self.lnb[:, (g * 8 + 7) * 128:(g * 8 + 8) * 128],
                    self.sident[:]), "ct")
                if g == 3:
                    aoe_free = tv
                r.wait("vector", "ct", tv)

                def ev(e, bi=bi, g=g, t=t):
                    dst = ap2(self.xT, g * 8 * 1024 + t * 128, [[1024, 8], [1, 128]])
                    src_ = ap2(self.pb[bi], 0, [[128, 8], [1, 128]])
                    return e.tensor_copy(out=dst, in_=src_)
                self.pb_free[bi] = r.do("vector", ev)
            aoT_cv[t] = self.pb_free[1]
        for o in range(8):
            mvs = {}
            for grp in range(2):
                tiles = list(range(grp * 4, grp * 4 + 4))
                for c in range(32):
                    q = c // 8
                    h = (c % 8) // 4
                    si = (o * 4 + q) % NWAR
                    if c % 4 == 0:
                        r.wait("tensor", *loaded[(o, q, h)])
                    for ti, t in enumerate(tiles):
                        bank = t % 4
                        if c == 0:
                            r.wait("tensor", "cv", aoT_cv[t])
                            r.wait("tensor", "cs", self.pf_free_cs[bank])
                            r.wait("tensor", "cv", self.pf_free_cv[bank])
                        last_mm = (c == 31)
                        hb = (grp == 1 and c % 4 == 3 and ti == 3)
                        fn = lambda e, c=c, t=t, si=si, bank=bank: e.matmul(
                            self.pfv(bank),
                            self.xT[:, c * 1024 + t * 128:c * 1024 + (t + 1) * 128],
                            self.warena[si][:, (c % 8) * 512:((c % 8) + 1) * 512],
                            start=(c == 0), stop=(c == 31))
                        if last_mm or hb:
                            v = r.inc("tensor", fn, "ct")
                            if last_mm:
                                mvs[t] = v
                            if hb:
                                ls = (o * 8 + q * 2 + h) % 8
                                self.half_free[ls] = v
                                if o + 1 < 8:
                                    loaded[(o + 1, q, h)] = self.load_half(
                                        self.Wop, o + 1, q, h)
                        else:
                            r.do("tensor", fn)
                for t in tiles:
                    self.emit_epilogue(t, o, t % 4, 1.0, mvs[t], resid_hx)
        return self.pop_tile_barriers()

    def emit_l1ffn(self, src, barrier_ac, resid_hx):
        r = self.r
        st = self.stat
        r.wait("gpsimd", "ct", r.cnt["ct"])
        r.inc("gpsimd", lambda e: e.dma_start(
            out=self.batt[:, 0:16384], in_=self.AwTp), "wlb", 16)
        r.inc("gpsimd", lambda e: e.dma_start(
            out=self.sE64, in_=self.E64d), "wlb", 16)
        r.inc("gpsimd", lambda e: e.dma_start(
            out=self.sWqTp, in_=self.WqTp), "wlb", 16)
        r.inc("gpsimd", lambda e: e.dma_start(
            out=self.skeysT, in_=self.keysT), "wlb", 16)
        wlb_all = r.cnt["wlb"]
        zgt_cv = [0] * NT
        for t, ap in self.ln_pipeline(src, barrier_ac):
            r.wait("tensor", "wlb", wlb_all)
            r.wait("tensor", "cv", self.xT_ready)
            r.wait("tensor", "cs", self.pf_free_cs[0])
            r.wait("tensor", "cv", self.pf_free_cv[0])
            z_ct = None
            for c in range(32):
                fn = lambda e, c=c, t=t: e.matmul(
                    self.pfv(0),
                    self.xT[:, c * 1024 + t * 128:c * 1024 + (t + 1) * 128],
                    self.batt[:, c * 512:(c + 1) * 512],
                    start=(c == 0), stop=(c == 31))
                if c == 31:
                    z_ct = r.inc("tensor", fn, "ct")
                else:
                    r.do("tensor", fn)
            r.wait("vector", "ct", z_ct)
            zev = r.do("vector", lambda e: e.tensor_copy(
                out=self.small1[:], in_=self.pfv(0)))
            self.pf_free_cv[0] = zev
            r.wait("tensor", "wlb", wlb_all)
            r.wait("tensor", "cs", self.pf_free_cs[1])
            r.wait("tensor", "cv", self.pf_free_cv[1])
            r.wait("tensor", "cv", self.xT_ready)
            qr_ct = None
            for c in range(32):
                fn = lambda e, c=c, t=t: e.matmul(
                    self.pfv(1)[:, 0:16],
                    self.xT[:, c * 1024 + t * 128:c * 1024 + (t + 1) * 128],
                    self.sWqTp[:, c * 16:(c + 1) * 16],
                    start=(c == 0), stop=(c == 31))
                if c == 31:
                    qr_ct = r.inc("tensor", fn, "ct")
                else:
                    r.do("tensor", fn)
            r.wait("vector", "ct", qr_ct)
            qe = r.do("vector", lambda e: e.tensor_copy(
                out=self.qrb[:], in_=self.pfv(1)[:, 0:16]))
            self.pf_free_cv[1] = qe
            r.wait("tensor", "cv", qe)
            r.wait("tensor", "cv", self.pb_free[1])
            qt_ct = r.inc("tensor", lambda e: e.transpose(
                self.pb[1][0:16, 0:128], self.qrb[:], self.sident[:]), "ct")
            r.wait("vector", "ct", qt_ct)
            qte = r.do("vector", lambda e: e.tensor_copy(
                out=self.qrT[:], in_=self.pb[1][0:16, 0:128]))
            self.pb_free[1] = qte
            r.wait("tensor", "cv", qte)
            r.wait("tensor", "cs", self.pf_free_cs[2])
            r.wait("tensor", "cv", self.pf_free_cv[2])
            sc_ct = r.inc("tensor", lambda e: e.matmul(
                self.pfv(2)[:, 0:64], self.qrT[:], self.skeysT[:],
                start=True, stop=True), "ct")
            r.wait("vector", "ct", sc_ct)
            sce = r.do("vector", lambda e: e.tensor_copy(
                out=self.l1s[:], in_=self.pfv(2)[:, 0:64]))
            self.pf_free_cv[2] = sce
            r.selfwait("vector")
            r.do("vector", lambda e: e.tensor_copy(out=self.l1c[:], in_=self.l1s[:]))
            r.selfwait("vector")
            r.do("vector", lambda e: e.max(out=self.sg8[:], in_=self.l1c[:]))
            r.selfwait("vector")
            r.do("vector", lambda e: e.match_replace(
                out=self.l1c[:], in_to_replace=self.sg8[:], in_values=self.l1c[:],
                imm_value=-1e30))
            r.selfwait("vector")
            r.do("vector", lambda e: e.max(out=self.sidxf[:], in_=self.l1c[:]))
            r.selfwait("vector")
            nmx = r.do("vector", lambda e: e.tensor_scalar(
                out=self.swm[:, 0:1], in0=self.sg8[:, 0:1], scalar1=-1.0,
                scalar2=None, op0=ALU.mult))
            r.wait("scalar", "cv", nmx)
            exv = r.do("scalar", lambda e: e.activation(
                out=self.l1c[:], in_=self.l1s[:], func=AF.Exp,
                bias=self.swm[:, 0:1]))
            r.wait("vector", "cs", exv)
            r.do("vector", lambda e: e.scalar_tensor_tensor(
                out=self.l1g[:], in0=self.l1s[:], scalar=self.sidxf[:, 7:8],
                in1=self.l1c[:], op0=ALU.is_ge, op1=ALU.mult,
                accum_out=self.swm[:, 1:2]))
            r.selfwait("vector")
            r.do("vector", lambda e: e.reciprocal(
                out=self.swm[:, 2:3], in_=self.swm[:, 1:2]))
            r.selfwait("vector")
            gbv = r.do("vector", lambda e: e.tensor_scalar(
                out=self.l1gb[:], in0=self.l1g[:], scalar1=self.swm[:, 2:3],
                scalar2=None, op0=ALU.mult))
            r.wait("tensor", "cv", gbv)
            r.wait("tensor", "cv", self.pb_free[0])
            gt_ct = r.inc("tensor", lambda e: e.transpose(
                self.pb[0][0:64, 0:128], self.l1gb[:], self.sident[:]), "ct")
            r.wait("vector", "ct", gt_ct)
            gte = r.do("vector", lambda e: e.tensor_copy(
                out=self.sgT[:], in_=self.pb[0][0:64, 0:128]))
            self.pb_free[0] = gte
            r.wait("tensor", "cv", gte)
            r.wait("tensor", "cs", self.pf_free_cs[3])
            r.wait("tensor", "cv", self.pf_free_cv[3])
            gf_ct = r.inc("tensor", lambda e: e.matmul(
                self.pfv(3), self.sgT[:], self.sE64[:],
                start=True, stop=True), "ct")
            r.wait("vector", "ct", gf_ct)
            zgv = r.do("vector", lambda e: e.tensor_tensor(
                out=self.small2[:], in0=self.small1[:], in1=self.pfv(3),
                op=ALU.mult))
            self.pf_free_cv[3] = zgv
            self.zg_cv = zgv
            r.wait("tensor", "cv", zgv)
            r.wait("tensor", "cv", self.pb_free[1])
            for j in range(3):
                r.do("tensor", lambda e, j=j: e.transpose(
                    self.pb[1][:, j * 128:(j + 1) * 128],
                    self.small2[:, j * 128:(j + 1) * 128], self.sident[:]))
            zgt_ct = r.inc("tensor", lambda e: e.transpose(
                self.pb[1][:, 384:512], self.small2[:, 384:512],
                self.sident[:]), "ct")
            r.wait("vector", "ct", zgt_ct)
            zge = r.do("vector", lambda e, t=t: e.tensor_copy(
                out=self.bb[:, t * 512:(t + 1) * 512], in_=self.pb[1][:, 0:512]))
            self.pb_free[1] = zge
            zgt_cv[t] = zge
        for o in range(8):
            si = o % NWAR
            r.wait("gpsimd", "ct", self.war_free[si])
            bsrc = self.BwT[:, o * 512:(o + 1) * 512].rearrange(
                "(c p) n -> p c n", p=128)
            wv = r.inc("gpsimd", lambda e, si=si, bsrc=bsrc: e.dma_start(
                out=ap2(self.warena[si], 0, [[512, 4], [1, 512]]),
                in_=bsrc), f"wl{si}", 16)
            for t in range(NT):
                bank = t % 4
                r.wait("tensor", f"wl{si}", wv)
                r.wait("tensor", "cv", zgt_cv[t])
                r.wait("tensor", "cs", self.pf_free_cs[bank])
                r.wait("tensor", "cv", self.pf_free_cv[bank])
                for j in range(3):
                    r.do("tensor", lambda e, t=t, j=j, si=si, bank=bank: e.matmul(
                        self.pfv(bank),
                        self.bb[:, t * 512 + j * 128:t * 512 + (j + 1) * 128],
                        self.warena[si][:, j * 512:(j + 1) * 512],
                        start=(j == 0), stop=False))
                mv = r.inc("tensor", lambda e, t=t, si=si, bank=bank: e.matmul(
                    self.pfv(bank),
                    self.bb[:, t * 512 + 384:t * 512 + 512],
                    self.warena[si][:, 1536:2048],
                    start=False, stop=True), "ct")
                self.emit_epilogue(t, o, bank, 1.0, mv, resid_hx)
            self.war_free[si] = r.cnt["ct"]
        return self.pop_tile_barriers()


    def emit_l1attn(self, src, barrier_ac, resid_hx):
        r = self.r
        st = self.stat
        r.wait("gpsimd", "ct", r.cnt["ct"])
        for t, ap in self.ln_pipeline(src, barrier_ac):
            r.wait("sync", "cs", ap)
            r.inc("sync", lambda e, t=t: e.dma_start(
                out=self.vv[t * 128:(t + 1) * 128, 0:4096], in_=self.lnb[:]),
                "g1", 16)
            self.lnb_store_g1 = r.inc("sync", lambda e, t=t: e.dma_start(
                out=self.vv[t * 128:(t + 1) * 128, 4096:8192], in_=self.lnb[:]),
                "g1", 16)
        vv_all = r.cnt["g1"]
        self.lnb_store_g1 = 0
        if getattr(self, "l1attn_phases", 4) < 2:
            return tuple(r.cnt[f"ac{i}"] for i in range(4))
        pt_cv = 0
        for kc in range(NKC):
            sic = (2 * kc) % NWAR
            sis = (2 * kc + 1) % NWAR
            r.wait("gpsimd", "ct", self.war_free[sic])
            lc = (f"wl{sic}", r.inc("gpsimd", lambda e, kc=kc, sic=sic: e.dma_start(
                out=self.warena[sic][:], in_=self.COSp[kc]), f"wl{sic}", 16))
            r.wait("gpsimd", "ct", self.war_free[sis])
            ls = (f"wl{sis}", r.inc("gpsimd", lambda e, kc=kc, sis=sis: e.dma_start(
                out=self.warena[sis][:], in_=self.SINp[kc]), f"wl{sis}", 16))
            r.wait("tensor", *lc)
            r.wait("tensor", *ls)
            r.wait("tensor", "cv", self.xT_ready)
            for g in range(2):
                for re_im, sl in ((0, sic), (1, sis)):
                    bank = g * 2 + re_im
                    r.wait("tensor", "cs", self.pf_free_cs[bank])
                    r.wait("tensor", "cv", self.pf_free_cv[bank])
                    mm_ct = None
                    for fc in range(32):
                        fn = lambda e, fc=fc, g=g, sl=sl, bank=bank: e.matmul(
                            self.pfv(bank),
                            self.warena[sl][:, fc * 128:(fc + 1) * 128],
                            self.xT[:, fc * 1024 + g * 512:fc * 1024 + g * 512 + 512],
                            start=(fc == 0), stop=(fc == 31))
                        if fc == 31:
                            mm_ct = r.inc("tensor", fn, "ct")
                        else:
                            r.do("tensor", fn)
                    self.last_fwd_ct = mm_ct
                r.wait("scalar", "ct", self.last_fwd_ct)
                r.wait("scalar", "cv", getattr(self, "evr_sq_cv", 0))
                s1 = r.do("scalar", lambda e, g=g: e.activation(
                    out=self.evr[:, 0:512], in_=self.pfv(g * 2), func=AF.Square))
                s2 = r.do("scalar", lambda e, g=g: e.activation(
                    out=self.evr[:, 512:1024], in_=self.pfv(g * 2 + 1),
                    func=AF.Square))
                self.pf_free_cs[g * 2] = s1
                self.pf_free_cs[g * 2 + 1] = s2
                r.wait("vector", "cs", s2)
                pt_cv = r.do("vector", lambda e, kc=kc, g=g: e.tensor_tensor(
                    out=self.batt[:, kc * 1024 + g * 512:kc * 1024 + g * 512 + 512],
                    in0=self.evr[:, 0:512], in1=self.evr[:, 512:1024],
                    op=ALU.add))
                self.evr_sq_cv = pt_cv
            self.war_free[sic] = r.cnt["ct"]
            self.war_free[sis] = r.cnt["ct"]
        if getattr(self, "l1attn_phases", 4) < 3:
            return tuple(r.cnt[f"ac{i}"] for i in range(4))
        corr_cv = [0] * NT
        for tc in range(32):
            si = tc % NWAR
            r.wait("gpsimd", "ct", self.war_free[si])
            li = (f"wl{si}", r.inc("gpsimd", lambda e, tc=tc, si=si: e.dma_start(
                out=self.warena[si][:, 0:KF], in_=self.ICOSTp[tc]), f"wl{si}", 16))
            r.wait("tensor", *li)
            if tc == 0:
                r.wait("tensor", "cv", pt_cv)
            for half in range(2):
                bank = (tc % 2) * 2 + half
                r.wait("tensor", "cs", self.pf_free_cs[bank])
                r.wait("tensor", "cv", self.pf_free_cv[bank])
                mm_ct = None
                for kc in range(NKC):
                    fn = lambda e, kc=kc, half=half, si=si, bank=bank: e.matmul(
                        self.pfv(bank),
                        self.warena[si][:, kc * 128:(kc + 1) * 128],
                        self.batt[:, kc * 1024 + half * 512:kc * 1024 + half * 512 + 512],
                        start=(kc == 0), stop=(kc == NKC - 1))
                    if kc == NKC - 1:
                        mm_ct = r.inc("tensor", fn, "ct")
                    else:
                        r.do("tensor", fn)
                r.wait("vector", "ct", mm_ct)
                r.wait("vector", "ct", getattr(self, "cbuf_free_ct", 0))
                ce = r.do("vector", lambda e, half=half, bank=bank: e.tensor_copy(
                    out=self.cbuf[:, half * 512:(half + 1) * 512],
                    in_=self.pfv(bank)))
                self.pf_free_cv[bank] = ce
                bi = half
                r.wait("tensor", "cv", ce)
                r.wait("tensor", "cv", self.pb_free[bi])
                tr_ct = None
                for tt in range(4):
                    fn = lambda e, half=half, tt=tt, bi=bi: e.transpose(
                        self.pb[bi][:, tt * 128:(tt + 1) * 128],
                        self.cbuf[:, half * 512 + tt * 128:half * 512 + (tt + 1) * 128],
                        self.sident[:])
                    if tt == 3:
                        tr_ct = r.inc("tensor", fn, "ct")
                    else:
                        r.do("tensor", fn)
                if half == 1:
                    self.cbuf_free_ct = tr_ct
                r.wait("vector", "ct", tr_ct)

                def ev(e, half=half, tc=tc, bi=bi):
                    dst = ap2(self.xT, (4 * half) * 4096 + tc * 128,
                              [[4096, 4], [1, 128]])
                    return e.tensor_copy(out=dst, in_=self.pb[bi][:, 0:512])
                cve = r.do("vector", ev)
                self.pb_free[bi] = cve
                for tt in range(4):
                    corr_cv[half * 4 + tt] = cve
            self.war_free[si] = r.cnt["ct"]
        if getattr(self, "l1attn_phases", 4) < 4:
            return tuple(r.cnt[f"ac{i}"] for i in range(4))
        acc_free = 0
        for t in range(NT):
            corr = self.xT[:, t * 4096:(t + 1) * 4096]
            r.wait("vector", "cv", corr_cv[t])
            r.selfwait("vector")
            r.do("vector", lambda e, corr=corr: e.max(
                out=self.l1gb[:, 0:8], in_=corr))
            r.selfwait("vector")
            mi = r.do("vector", lambda e, corr=corr: e.max_index(
                out=self.sgidx[:], in_max=self.l1gb[:, 0:8], in_values=corr))
            r.selfwait("vector")
            nv = r.do("vector", lambda e: e.tensor_scalar(
                out=st[:, 173:174], in0=self.l1gb[:, 0:1], scalar1=-1.0,
                scalar2=None, op0=ALU.mult))
            r.wait("scalar", "cv", nv)
            exv = r.do("scalar", lambda e: e.activation(
                out=self.sg8[:], in_=self.l1gb[:, 0:8], func=AF.Exp,
                bias=st[:, 173:174], accum_out=st[:, 174:175]))
            r.wait("vector", "cs", exv)
            r.do("vector", lambda e: e.reciprocal(
                out=st[:, 175:176], in_=st[:, 174:175]))
            r.selfwait("vector")
            r.do("vector", lambda e: e.tensor_scalar(
                out=self.sidxf[:], in0=self.sg8[:], scalar1=st[:, 175:176],
                scalar2=None, op0=ALU.mult))
            r.do("vector", lambda e: e.tensor_copy(
                out=self.l1s[:, 0:8], in_=self.sgidx[:]))
            r.selfwait("vector")
            r.do("vector", lambda e: e.tensor_scalar(
                out=self.l1s[:, 8:16], in0=self.l1s[:, 0:8],
                scalar1=self.siotavv[:, 0:1], scalar2=None, op0=ALU.add))
            r.selfwait("vector")
            gi = r.do("vector", lambda e: e.tensor_copy(
                out=self.sgidx[:], in_=self.l1s[:, 8:16]))
            r.wait("gpsimd", "cv", gi)
            r.wait("gpsimd", "g1", vv_all)
            gsem_val = {}
            applied = {}

            def gath(j):
                gb = self.bb if j % 2 == 0 else self.lnb
                gs = "ga0" if j % 2 == 0 else "qt"
                if j >= 2:
                    r.wait("gpsimd", "cv", applied[j - 2])
                gsem_val[j] = r.inc("gpsimd",
                    lambda e, j=j, t=t, gb=gb: e.indirect_dma_start(
                        out=gb[:], out_offset=None, in_=self.vv,
                        in_offset=IndirectOffsetOnAxis(self.sgidx[:, j:j + 1], 1),
                        element_offset=t * 128 * 8192), gs, 16)

            def appl(j):
                gb = self.bb if j % 2 == 0 else self.lnb
                gs = "ga0" if j % 2 == 0 else "qt"
                r.wait("vector", gs, gsem_val[j])
                if j == 0:
                    r.wait("vector", "ac0", getattr(self, "acc_ac", 0))
                    av = r.do("vector", lambda e, gb=gb: e.tensor_scalar(
                        out=self.xgb[0][:], in0=gb[:], scalar1=self.sidxf[:, 0:1],
                        scalar2=None, op0=ALU.mult))
                else:
                    r.selfwait("vector")
                    av = r.do("vector", lambda e, gb=gb, j=j: e.scalar_tensor_tensor(
                        out=self.xgb[0][:], in0=gb[:], scalar=self.sidxf[:, j:j + 1],
                        in1=self.xgb[0][:], op0=ALU.mult, op1=ALU.add))
                applied[j] = av

            lvl = getattr(self, "l1attn_phases", 6)
            if lvl < 5:
                continue
            gath(0)
            gath(1)
            if lvl < 6:
                continue
            for j in range(8):
                appl(j)
                if j + 2 < 8:
                    gath(j + 2)
            av = applied[7]
            if getattr(self, "l1attn_phases", 7) < 7:
                continue
            r.wait("gpsimd", "cv", av)
            if resid_hx:
                r.wait("gpsimd", "hr", resid_hx)
            for ch in range(4):
                self.acc_ac = r.inc("gpsimd", lambda e, t=t, ch=ch: e.dma_start(
                    out=self.resid[t * 128:(t + 1) * 128, ch * 1024:(ch + 1) * 1024],
                    in_=self.xgb[0][:, ch * 1024:(ch + 1) * 1024],
                    accum_op=ALU.add), "ac0", 16)
            self.tile_acc.setdefault(t, []).append(("ac0", self.acc_ac))
        self.xf_extra[0] = [("ac0", r.cnt["ac0"])]
        self.lnb_gate = [("qt", r.cnt["qt"]), ("cv", r.cnt["cv"])]
        return self.pop_tile_barriers()

    def emit_final(self, barrier_ac):
        r = self.r
        if barrier_ac:
            for t, lst in sorted(barrier_ac.items()):
                for sem, val in lst:
                    r.wait("sync", sem, val)
        for t in range(NT):
            r.inc("sync", lambda e, t=t: e.dma_start(
                out=self.y[t * 128:(t + 1) * 128, :],
                in_=self.resid[t * 128:(t + 1) * 128, :]), "au", 16)
        r.wait("sync", "au", r.cnt["au"])
        for i in range(4):
            r.wait("gpsimd", f"ac{i}", r.cnt[f"ac{i}"])
        for i in range(8):
            r.wait("gpsimd", f"wl{i}", r.cnt[f"wl{i}"])
        r.wait("gpsimd", "wlb", r.cnt["wlb"])
        r.wait("vector", "cv", r.cnt["cv"])
        r.wait("scalar", "cs", r.cnt["cs"])
        r.wait("tensor", "ct", r.cnt["ct"])


def build_program(parts, core_id=0):
    m = MK(parts, core_id)
    m.emit_const_loads()
    m.emit_resid_init()
    bar = None
    first = True
    for p in parts:
        src = m.xin if first else m.resid
        if p.startswith("l0ffn"):
            bar = m.emit_l0ffn(src, bar, m.resid_init_hx)
        elif p.startswith("l0attn"):
            bar = m.emit_l0attn(src, bar, m.resid_init_hx)
        elif p.startswith("l1ffn"):
            bar = m.emit_l1ffn(src, bar, m.resid_init_hx)
        elif p.startswith("l1attn"):
            if "@" in p:
                m.l1attn_phases = int(p.split("@")[1])
            bar = m.emit_l1attn(src, bar, m.resid_init_hx)
        else:
            raise ValueError(p)
        first = False
    m.emit_final(bar)
    m.r.replay()
    m.close()
    return m



PARTS = ["l0ffn", "l0attn", "l0ffn2", "l1ffn", "l1attn", "l1ffn2"]


def kernel(**inputs):
    x0 = np.ascontiguousarray(
        np.asarray(inputs["L0_attn_x"], dtype=np.float32).reshape(NCHIP * TOK, D))
    consts = make_consts()
    prog = build_program(PARTS)
    declared = set()
    for alloc in prog.nc.m.functions[0].allocations:
        if isinstance(alloc, mybir.MemoryLocationSet) and alloc.kind == "ExternalInput":
            declared.add(alloc.memorylocations[0].name)
    in_maps = []
    for c in range(NCHIP):
        w = pack_weights(inputs, c)
        m = dict(xin=np.ascontiguousarray(x0[core_rows(c)]), **w, **consts)
        in_maps.append({k: v for k, v in m.items() if k in declared})
    res = run_bass_kernel_spmd(prog.nc, in_maps, core_ids=list(range(NCHIP)))
    out = np.empty((NCHIP * TOK, D), dtype=np.float32)
    for c in range(NCHIP):
        out[core_rows(c)] = res.results[c]["y"]
    return out.reshape(2, 4096, D)
